# version 05 (final)
# baseline (speedup 1.0000x reference)
import numpy as np
import ml_dtypes
from contextlib import ExitStack

import concourse.bass as bass
import concourse.mybir as mybir
from concourse.bass_utils import run_bass_kernel_spmd

F32 = mybir.dt.float32
BF16 = mybir.dt.bfloat16
AF = mybir.ActivationFunctionType
ALU = mybir.AluOpType
AX = mybir.AxisListType
BF = ml_dtypes.bfloat16

ENGINES = ("tensor", "vector", "scalar", "gpsimd", "sync")
QUEUES = ("sync", "scalar", "gpsimd", "vector")
D_MODEL = 1024
NCH = D_MODEL // 128


class Prog:
    def __init__(self, nc):
        self.nc = nc
        self.records = []
        self.last_writer = {}
        self.readers = {}
        self.eng_pos = {e: 0 for e in ENGINES}
        self.dma_count = {q: 0 for q in QUEUES}

    def op(self, eng, fn, reads=(), writes=(), dma=False, tag=""):
        idx = len(self.records)
        deps = set()
        for k in reads:
            w = self.last_writer.get(k)
            if w is not None:
                deps.add(w)
        for k in writes:
            w = self.last_writer.get(k)
            if w is not None:
                deps.add(w)
            for r in self.readers.get(k, ()):
                deps.add(r)
        deps.discard(idx)
        rec = dict(
            idx=idx, eng=eng, fn=fn, deps=sorted(deps), dma=dma, tag=tag,
            eng_pos=self.eng_pos[eng],
            dma_idx=self.dma_count[eng] if dma else None,
            published=False, need_flush=False, pubno=None,
        )
        self.records.append(rec)
        self.eng_pos[eng] += 1
        if dma:
            self.dma_count[eng] += 1
        for k in reads:
            self.readers.setdefault(k, []).append(idx)
        for k in writes:
            self.last_writer[k] = idx
            self.readers[k] = []
        return idx

    def dma(self, queue, out, in_, reads=(), writes=(), tag="", **kw):
        return self.op(
            queue, lambda e: e.dma_start(out=out, in_=in_, **kw),
            reads=reads, writes=writes, dma=True, tag=tag)

    def barrier_key(self):
        return "__barrier__"

    def finalize_and_emit(self, block, sems, dma_sems):
        recs = self.records
        for r in recs:
            for d in r["deps"]:
                p = recs[d]
                if p["dma"]:
                    p["need_flush"] = True
                elif p["eng"] != r["eng"]:
                    p["published"] = True
        pubno = {e: 0 for e in ENGINES}
        for r in recs:
            if r["published"]:
                pubno[r["eng"]] += 1
                r["pubno"] = pubno[r["eng"]]
        flush_after = {q: {} for q in QUEUES}
        for r in recs:
            if r["dma"] and r["need_flush"]:
                flush_after[r["eng"]][r["dma_idx"]] = 16 * (r["dma_idx"] + 1)

        per_eng = {e: [r for r in recs if r["eng"] == e] for e in ENGINES}

        def emit_engine(engobj, ename):
            waited = {e: 0 for e in ENGINES}
            waited_q = {q: 0 for q in QUEUES}
            last_drain_cover = -1
            for r in per_eng[ename]:
                need_drain = False
                for d in r["deps"]:
                    p = recs[d]
                    if p["dma"]:
                        q = p["eng"]
                        val = flush_after[q][p["dma_idx"]]
                        if waited_q[q] < val:
                            engobj.wait_ge(dma_sems[q], val)
                            waited_q[q] = val
                    elif p["eng"] == ename:
                        if ename != "tensor" and p["eng_pos"] > last_drain_cover:
                            need_drain = True
                    else:
                        val = p["pubno"]
                        if waited[p["eng"]] < val:
                            engobj.wait_ge(sems[p["eng"]], val)
                            waited[p["eng"]] = val
                if need_drain:
                    engobj.drain()
                    last_drain_cover = r["eng_pos"] - 1
                inst = r["fn"](engobj)
                if r["dma"]:
                    inst.then_inc(dma_sems[ename], 16)
                    fv = flush_after[ename].get(r["dma_idx"])
                    if fv is not None:
                        engobj.wait_ge(dma_sems[ename], fv)
                        waited_q[ename] = fv
                elif r["published"]:
                    if ename == "tensor":
                        inst.then_inc(sems[ename], 1)
                    else:
                        engobj.drain().then_inc(sems[ename], 1)
                        last_drain_cover = r["eng_pos"]
            if ename in QUEUES:
                total = 16 * self.dma_count[ename]
                if total > 0 and waited_q[ename] < total:
                    engobj.wait_ge(dma_sems[ename], total)

        for ename in ENGINES:
            getattr(block, ename)(lambda engobj, _n=ename: emit_engine(engobj, _n))


def cdiv(a, b):
    return (a + b - 1) // b


class Env:

    def __init__(self, nc, prog, es):
        self.nc = nc
        self.prog = prog
        self.es = es
        self.sb = {}
        self.ps = {}
        self.dram = {}
        self._uniq = 0

    def sbuf(self, name, shape, dtype):
        key = (name, tuple(shape), dtype)
        if key not in self.sb:
            self.sb[key] = self.es.enter_context(
                self.nc.sbuf_tensor(f"sb_{name}", list(shape), dtype))
        return self.sb[key]

    def psum(self, name, shape, dtype=F32):
        key = (name, tuple(shape), dtype)
        if key not in self.ps:
            self.ps[key] = self.es.enter_context(
                self.nc.psum_tensor(f"ps_{name}", list(shape), dtype))
        return self.ps[key]

    def dram_scratch(self, name, shape, dtype):
        if name not in self.dram:
            self.dram[name] = self.nc.dram_tensor(f"dr_{name}", list(shape), dtype)
        return self.dram[name]

    def uniq(self):
        self._uniq += 1
        return self._uniq


def emit_ln_tile(env, xt, xt_key, ht, ht_key, stats, stats_key, scratch, scratch_key, D):
    eps = env.constpool[:, 0:1]
    if isinstance(xt_key, list):
        xt_keys = xt_key
    else:
        xt_keys = [xt_key]
    """LayerNorm (no affine) of token-major tile xt [128, D] f32 -> ht bf16.

    stats: f32 sbuf [128, 8] scratch for per-row stats.
    scratch: f32 sbuf [128, D] (squares dump).
    """
    p = env.prog
    s_sum = stats[:, 0:1]
    s_ss = stats[:, 1:2]
    s_negm = stats[:, 2:3]
    s_var = stats[:, 3:4]
    s_rstd = stats[:, 4:5]
    s_tmp = stats[:, 5:6]
    p.op("vector", lambda v: v.reduce_sum(s_sum, xt[:], AX.X),
         reads=xt_keys, writes=[(stats_key, 0)])
    p.op("scalar", lambda s: s.activation(scratch[:], xt[:], AF.Square,
                                          accum_out=s_ss),
         reads=xt_keys, writes=[scratch_key, (stats_key, 1)])
    p.op("vector", lambda v: v.tensor_scalar(s_negm, s_sum, -1.0 / D, None, ALU.mult),
         reads=[(stats_key, 0)], writes=[(stats_key, 2)])
    p.op("vector", lambda v: v.tensor_tensor(s_tmp, s_negm, s_negm, ALU.mult),
         reads=[(stats_key, 2)], writes=[(stats_key, 5)])
    p.op("vector", lambda v: v.tensor_scalar(s_var, s_ss, 1.0 / D, s_tmp,
                                             ALU.mult, ALU.subtract),
         reads=[(stats_key, 1), (stats_key, 5)], writes=[(stats_key, 3)])
    p.op("scalar", lambda s: s.activation(s_tmp, s_var, AF.Ln, bias=eps),
         reads=[(stats_key, 3), "constpool"], writes=[(stats_key, 5)])
    p.op("scalar", lambda s: s.activation(s_rstd, s_tmp, AF.Exp, scale=-0.5),
         reads=[(stats_key, 5)], writes=[(stats_key, 4)])
    p.op("vector", lambda v: v.tensor_scalar(ht[:], xt[:], s_negm, s_rstd,
                                             ALU.add, ALU.mult),
         reads=xt_keys + [(stats_key, 2), (stats_key, 4)], writes=[ht_key])


def emit_transpose_tile(env, src_bf16, src_key, dst, dst_key, ident, ncols):
    p = env.prog
    tps = env.psum("tpp", [128, 512], BF16)
    for g in range(cdiv(ncols, 4)):
        cs = range(g * 4, min(ncols, g * 4 + 4))
        for i, c in enumerate(cs):
            p.op("tensor",
                 lambda t, c=c, i=i: t.transpose(
                     tps[:, i * 128:(i + 1) * 128],
                     src_bf16[:, c * 128:(c + 1) * 128], ident[:]),
                 reads=[src_key, "ident"], writes=["tpp"])
        n = len(cs) * 128
        p.op("scalar",
             lambda s, g=g, n=n: s.activation(
                 dst[:, g * 512:g * 512 + n], tps[:, 0:n], AF.Copy),
             reads=["tpp"], writes=[(dst_key, g)])


def mm_chain(env, ps, ps_key, pairs, extra_reads=(), tag="mm", start_first=True,
             stop_last=True):
    p = env.prog
    n = len(pairs)
    for i, (lhsT, rhs, rkeys) in enumerate(pairs):
        p.op("tensor",
             lambda t, lhsT=lhsT, rhs=rhs, i=i: t.matmul(
                 ps, lhsT, rhs, start=(i == 0 and start_first),
                 stop=(i == n - 1 and stop_last)),
             reads=list(rkeys) + list(extra_reads), writes=[ps_key], tag=tag)


def emit_l1_ffn_prep(env, w):
    p = env.prog
    nc = env.nc
    weff = env.sbuf("l1ffn_weff", [128, NCH * D_MODEL], BF16)
    lu = env.sbuf("l1ffn_lu", [50, 4], BF16)
    ua = env.sbuf("l1ffn_ua", [50, 1024], BF16)
    ldn = env.sbuf("l1ffn_ld", [4, 100], BF16)
    da = env.sbuf("l1ffn_da", [100, 1024], BF16)
    a_sb = env.sbuf("l1ffn_a", [4, 1024], BF16)
    b_sb = env.sbuf("l1ffn_bmat", [100, 1024], BF16)
    bias_row = env.sbuf("l1ffn_bias", [1, 1024], BF16)
    wt_f32 = env.sbuf("l1ffn_wt32", [128, 1024], F32)

    p.dma("gpsimd", lu[:], w["lora_up"][:], writes=["lu"])
    p.dma("gpsimd", ua[:], w["up_aux_T"][:], writes=["ua"])
    p.dma("gpsimd", ldn[:], w["lora_down"][:], writes=["ld"])
    p.dma("gpsimd", da[:], w["down_aux"][:], writes=["da"])
    p.dma("gpsimd", bias_row[:], w["b"][:], writes=["l1bias"])

    pa = env.psum("pp0", [128, 512], F32)
    pb = env.psum("pp1", [128, 512], F32)
    for half in range(2):
        sl = slice(half * 512, half * 512 + 512)
        mm_chain(env, pa[0:4, 0:512], "pp0",
                 [(lu[:], ua[:, sl], ["lu", "ua"])])
        p.op("scalar", lambda s, sl=sl, half=half: s.activation(
            a_sb[:, sl], pa[0:4, 0:512], AF.Copy),
            reads=["pp0"], writes=[("a_sb", half)])
    for half in range(2):
        sl = slice(half * 512, half * 512 + 512)
        mm_chain(env, pb[0:100, 0:512], "pp1",
                 [(ldn[:], a_sb[:, sl], ["ld", ("a_sb", half)])])
        p.op("scalar", lambda s, sl=sl, half=half: s.activation(
            b_sb[:, sl], pb[0:100, 0:512], AF.Copy),
            reads=["pp1"], writes=[("b_sb", half)])
    for c in range(NCH):
        p.dma("sync", wt_f32[:], w["Wt"][c * 128:(c + 1) * 128, :],
              writes=["wt32"])
        for half in range(2):
            sl = slice(half * 512, half * 512 + 512)
            mm_chain(env, pa[:, 0:512], "pp0",
                     [(da[:, c * 128:(c + 1) * 128], b_sb[:, sl],
                       ["da", ("b_sb", half)])])
            p.op("vector", lambda v, c=c, sl=sl, half=half: v.tensor_tensor(
                weff[:, c * 1024:(c + 1) * 1024][:, sl],
                pa[:, 0:512], wt_f32[:, sl], ALU.add),
                reads=["pp0", "wt32"], writes=[("weff", c)])
    return weff, bias_row


def emit_l1_ffn(env, cfg, x_in, x_out, weff, bias_row, ident, ones_col):
    p = env.prog
    T = cfg["T"]
    nt = T // 128
    u = env.uniq()
    xt = [env.sbuf(f"xt{i}", [128, D_MODEL], F32) for i in range(2)]
    ht = [env.sbuf(f"ht{i}", [128, D_MODEL], BF16) for i in range(2)]
    hTt = [env.sbuf(f"hTt{i}", [128, D_MODEL], BF16) for i in range(2)]
    xo = [env.sbuf(f"xo{i}", [128, D_MODEL], F32) for i in range(2)]
    stats = env.sbuf("stats", [128, 16], F32)
    scratch = [env.sbuf(f"lnscratch{i}", [128, D_MODEL], F32) for i in range(2)]
    yps = [env.psum(f"yps{i}", [128, 512], F32) for i in range(2)]

    for tt in range(nt):
        b = tt % 2
        ks = (u, "l1f", tt)
        p.dma("sync", xt[b][:], x_in[tt * 128:(tt + 1) * 128, :],
              writes=[(f"xt{b}",)], tag=f"x{tt}")
        emit_ln_tile(env, xt[b], (f"xt{b}",), ht[b], (f"ht{b}",),
                     stats[:, 8 * b:8 * b + 8], f"st{b}", scratch[b],
                     f"lnscratch{b}", D_MODEL)
        emit_transpose_tile(env, ht[b], (f"ht{b}",), hTt[b], f"hT{b}", ident, NCH)
        for half in range(2):
            sl = slice(half * 512, half * 512 + 512)
            pairs = []
            for c in range(NCH):
                pairs.append((hTt[b][:, c * 128:(c + 1) * 128],
                              weff[:, c * 1024:(c + 1) * 1024][:, sl],
                              [(f"hT{b}", c // 4), ("weff", c)]))
            mm_chain(env, yps[half][:], (f"yps{half}",), pairs, tag="l1f_mm",
                     stop_last=False)
            p.op("tensor", lambda t, half=half, sl=sl: t.matmul(
                yps[half][:], ones_col[:], bias_row[:, sl],
                start=False, stop=True),
                reads=["onescol", "l1bias"], writes=[(f"yps{half}",)])
            p.op("vector", lambda v, half=half, sl=sl, b=b: v.tensor_tensor(
                xo[b][:, sl], yps[half][:], xt[b][:, sl], ALU.add),
                reads=[(f"yps{half}",), (f"xt{b}",)], writes=[(f"xo{b}", half)])
        p.dma("scalar", x_out[tt * 128:(tt + 1) * 128, :], xo[b][:],
              reads=[(f"xo{b}", 0), (f"xo{b}", 1)], writes=[("xout", tt)],
              tag=f"xo{tt}")


def host_constants():
    return {
        "c_ident": np.eye(128).astype(BF),
        "c_ones_col": np.ones((1, 128), dtype=BF),
        "c_tril": np.tril(np.ones((128, 128))).astype(np.float32),
    }


def declare_const_inputs(nc, env):
    p = env.prog
    cp = env.sbuf("constpool", [128, 8], F32)
    env.constpool = cp
    p.op("vector", lambda v: v.memset(cp[:, 0:1], 1e-5), writes=["constpool"])
    ident_dr = nc.dram_tensor("c_ident", [128, 128], BF16, kind="ExternalInput").ap()
    ones_dr = nc.dram_tensor("c_ones_col", [1, 128], BF16, kind="ExternalInput").ap()
    ident = env.sbuf("ident", [128, 128], BF16)
    ones_col = env.sbuf("ones_col", [1, 128], BF16)
    p.dma("sync", ident[:], ident_dr[:], writes=["ident"])
    p.dma("sync", ones_col[:], ones_dr[:], writes=["onescol"])
    return ident, ones_col


def build_l1_ffn_program(T):
    nc = bass.Bass("TRN2", target_bir_lowering=False)
    cfg = {"T": T}
    x_in = nc.dram_tensor("x_in", [T, D_MODEL], F32, kind="ExternalInput").ap()
    x_out = nc.dram_tensor("x_out", [T, D_MODEL], F32, kind="ExternalOutput").ap()
    w = {
        "Wt": nc.dram_tensor("Wt", [1024, 1024], F32, kind="ExternalInput").ap(),
        "lora_up": nc.dram_tensor("lora_up", [50, 4], F32, kind="ExternalInput").ap(),
        "up_aux_T": nc.dram_tensor("up_aux_T", [50, 1024], F32, kind="ExternalInput").ap(),
        "lora_down": nc.dram_tensor("lora_down", [4, 100], F32, kind="ExternalInput").ap(),
        "down_aux": nc.dram_tensor("down_aux", [100, 1024], F32, kind="ExternalInput").ap(),
        "b": nc.dram_tensor("bvec", [1, 1024], F32, kind="ExternalInput").ap(),
    }
    with ExitStack() as es:
        prog = Prog(nc)
        env = Env(nc, prog, es)
        sems = {e: es.enter_context(nc.semaphore(f"s_{e}")) for e in ENGINES}
        dma_sems = {q: es.enter_context(nc.semaphore(f"d_{q}")) for q in QUEUES}
        ident, ones_col = declare_const_inputs(nc, env)
        weff, bias_row = emit_l1_ffn_prep(env, w)
        emit_l1_ffn(env, cfg, x_in, x_out, weff, bias_row, ident, ones_col)
        prog.op("sync", lambda e: e.nop(),
                reads=[("xout", tt) for tt in range(T // 128)])
        block = es.enter_context(nc.Block())
        prog.finalize_and_emit(block, sems, dma_sems)
    return nc


def kernel(**inputs):
    raise NotImplementedError("assembled later")


def emit_l1_attn_b(env, cfg, x_in, x_out, w, ident, ones_col):
    p = env.prog
    B, L = cfg["B"], cfg["L"]
    ntb = L // 128
    xt = [env.sbuf(f"xt{i}", [128, D_MODEL], F32) for i in range(2)]
    ht = [env.sbuf(f"ht{i}", [128, D_MODEL], BF16) for i in range(2)]
    hTt = [env.sbuf(f"hTt{i}", [128, D_MODEL], BF16) for i in range(2)]
    xo = [env.sbuf(f"xo{i}", [128, D_MODEL], F32) for i in range(2)]
    stats = env.sbuf("stats", [128, 16], F32)
    scratch = [env.sbuf(f"lnscratch{i}", [128, D_MODEL], F32) for i in range(2)]

    wq = env.sbuf("qb_wq", [128, NCH * D_MODEL], BF16)
    wk = env.sbuf("qb_wk", [128, NCH * D_MODEL], BF16)
    wv = env.sbuf("qb_wv", [128, NCH * D_MODEL], BF16)
    for c in range(NCH):
        p.dma("gpsimd", wq[:, c * 1024:(c + 1) * 1024],
              w["Wq"][c * 128:(c + 1) * 128, :], writes=[("qb_wq", c)])
        p.dma("gpsimd", wk[:, c * 1024:(c + 1) * 1024],
              w["Wk"][c * 128:(c + 1) * 128, :], writes=[("qb_wk", c)])
        p.dma("gpsimd", wv[:, c * 1024:(c + 1) * 1024],
              w["Wv"][c * 128:(c + 1) * 128, :], writes=[("qb_wv", c)])
    bqc = env.sbuf("qb_bq", [128, NCH], F32)
    bkc = env.sbuf("qb_bk", [128, NCH], F32)
    bvr = env.sbuf("qb_bv", [1, D_MODEL], BF16)
    p.dma("gpsimd", bqc[:], w["bq"].rearrange("(c p) -> p c", p=128),
          writes=["qb_bq"], allow_slow_non_contiguous=True)
    p.dma("gpsimd", bkc[:], w["bk"].rearrange("(c p) -> p c", p=128),
          writes=["qb_bk"], allow_slow_non_contiguous=True)
    p.dma("gpsimd", bvr[:], w["bv"].rearrange("(o d) -> o d", o=1), writes=["qb_bv"])

    tril = env.sbuf("tril", [128, 128], F32)
    p.dma("sync", tril[:], env.tril_dram[:], writes=["tril"])

    qT = env.sbuf("qb_qT", [128, NCH * L], BF16)
    kT = env.sbuf("qb_kT", [128, NCH * L], BF16)
    vsb = env.sbuf("qb_v", [128, ntb * D_MODEL], BF16)
    pp = env.psum("pp0", [128, 512], F32)
    pq = env.psum("pp1", [128, 512], F32)
    srow = env.sbuf("qb_srow", [128, 2048], F32)
    erow = env.sbuf("qb_erow", [128, 2048], F32)
    st2 = env.sbuf("qb_st", [128, 16], F32)

    scale = 1.0 / 32.0

    for b in range(B):
        for it in range(ntb):
            par = it % 2
            row0 = b * L + it * 128
            p.dma("sync", xt[par][:], x_in[row0:row0 + 128, :], writes=[(f"xt{par}",)])
            emit_ln_tile(env, xt[par], (f"xt{par}",), ht[par], (f"ht{par}",),
                         stats[:, 8 * par:8 * par + 8], f"st{par}",
                         scratch[par], f"lnscratch{par}", D_MODEL)
            emit_transpose_tile(env, ht[par], (f"ht{par}",), hTt[par], f"hT{par}",
                                ident, NCH)
            hkeys = [(f"hT{par}", g) for g in range(2)]
            for (wt, wkey, dst, dkey, bias, biaskey) in (
                    (wq, "qb_wq", qT, "qT", bqc, "qb_bq"),
                    (wk, "qb_wk", kT, "kT", bkc, "qb_bk")):
                for oc in range(NCH):
                    pairs = []
                    for dc in range(NCH):
                        pairs.append((wt[:, dc * 1024 + oc * 128:dc * 1024 + oc * 128 + 128],
                                      hTt[par][:, dc * 128:(dc + 1) * 128],
                                      [(wkey, dc)] + hkeys))
                    mm_chain(env, pq[:, 0:128], "pp1", pairs, tag="qk")
                    p.op("scalar",
                         lambda s, dst=dst, oc=oc, it=it, bias=bias: s.activation(
                             dst[:, oc * L + it * 128: oc * L + it * 128 + 128],
                             pq[:, 0:128], AF.Identity, bias=bias[:, oc:oc + 1]),
                         reads=["pp1", biaskey], writes=[(dkey, oc, it)])
            for half in range(2):
                pairs = []
                for dc in range(NCH):
                    pairs.append((hTt[par][:, dc * 128:(dc + 1) * 128],
                                  wv[:, dc * 1024 + half * 512:dc * 1024 + half * 512 + 512],
                                  [("qb_wv", dc)] + hkeys))
                mm_chain(env, pp[:], "pp0", pairs, stop_last=False, tag="v")
                p.op("tensor", lambda t, half=half: t.matmul(
                    pp[:], ones_col[:], bvr[:, half * 512:half * 512 + 512],
                    start=False, stop=True),
                    reads=["onescol", "qb_bv"], writes=["pp0"])
                p.op("scalar", lambda s, it=it, half=half: s.activation(
                    vsb[:, it * 1024 + half * 512: it * 1024 + half * 512 + 512],
                    pp[:], AF.Copy),
                    reads=["pp0"], writes=[("v", it, half)])
        for it in range(ntb):
            par = it % 2
            ncols = (it + 1) * 128
            njc = cdiv(ncols, 512)
            for jc in range(njc):
                n = min(512, ncols - jc * 512)
                pairs = []
                for dc in range(NCH):
                    pairs.append((qT[:, dc * L + it * 128: dc * L + it * 128 + 128],
                                  kT[:, dc * L + jc * 512: dc * L + jc * 512 + n],
                                  [("qT", dc, it)] +
                                  [("kT", dc, j4) for j4 in
                                   range(jc * 4, min(ntb, jc * 4 + cdiv(n, 128)))]))
                mm_chain(env, pq[:, 0:n], "pp1", pairs, tag="sc")
                p.op("scalar", lambda s, jc=jc, n=n: s.activation(
                    srow[:, jc * 512:jc * 512 + n], pq[:, 0:n], AF.Copy,
                    scale=scale),
                    reads=["pp1"], writes=[("srow", jc)])
            p.op("vector", lambda v, it=it: v.tensor_tensor(
                srow[:, it * 128:it * 128 + 128],
                srow[:, it * 128:it * 128 + 128], tril[:], ALU.mult),
                reads=[("srow", it // 4), "tril"], writes=[("srow", "diag")])
            s_m = st2[:, 0:1]
            s_negm = st2[:, 1:2]
            s_se = st2[:, 2:3]
            s_emn = st2[:, 3:4]
            s_z = st2[:, 4:5]
            s_rz = st2[:, 5:6]
            s_sa = st2[:, 6:7]
            rd = [("srow", jc) for jc in range(njc)] + [("srow", "diag")]
            p.op("vector", lambda v, ncols=ncols: v.tensor_reduce(
                s_m, srow[:, 0:ncols], AX.X, ALU.max),
                reads=rd, writes=[("st2", 0)])
            c_t = L - ncols
            if c_t > 0:
                p.op("vector", lambda v: v.tensor_scalar(
                    s_m, s_m, 0.0, None, ALU.max),
                    reads=[("st2", 0)], writes=[("st2", 0)])
            p.op("vector", lambda v: v.tensor_scalar(
                s_negm, s_m, -1.0, None, ALU.mult),
                reads=[("st2", 0)], writes=[("st2", 1)])
            p.op("scalar", lambda s, ncols=ncols: s.activation(
                erow[:, 0:ncols], srow[:, 0:ncols], AF.Exp, bias=s_negm,
                accum_out=s_se),
                reads=rd + [("st2", 1)], writes=["erow", ("st2", 2)])
            p.op("scalar", lambda s: s.activation(s_emn, s_m, AF.Exp, scale=-1.0),
                 reads=[("st2", 0)], writes=[("st2", 3)])
            p.op("vector", lambda v, c_t=c_t: v.scalar_tensor_tensor(
                s_z, s_emn, float(c_t), s_se, ALU.mult, ALU.add),
                reads=[("st2", 2), ("st2", 3)], writes=[("st2", 4)])
            p.op("vector", lambda v: v.reciprocal(s_rz, s_z),
                 reads=[("st2", 4)], writes=[("st2", 5)])
            p.op("vector", lambda v: v.tensor_tensor(s_sa, s_z, s_rz, ALU.mult),
                 reads=[("st2", 4), ("st2", 5)], writes=[("st2", 6)])
            row0 = b * L + it * 128
            p.dma("sync", xt[par][:], x_in[row0:row0 + 128, :], writes=[(f"xt{par}",)])
            p.op("vector", lambda v, it=it, par=par: v.scalar_tensor_tensor(
                xo[par][:], vsb[:, it * 1024:(it + 1) * 1024], s_sa, xt[par][:],
                ALU.mult, ALU.add),
                reads=[(f"xt{par}",), ("st2", 6), ("v", it, 0), ("v", it, 1)],
                writes=[(f"xo{par}", 0), (f"xo{par}", 1)])
            p.dma("scalar", x_out[row0:row0 + 128, :], xo[par][:],
                  reads=[(f"xo{par}", 0), (f"xo{par}", 1)],
                  writes=[("xout", b, it)], tag="xo")


def build_l1_attn_b_program(B, L):
    nc = bass.Bass("TRN2", target_bir_lowering=False)
    cfg = {"B": B, "L": L, "T": B * L}
    T = B * L
    x_in = nc.dram_tensor("x_in", [T, D_MODEL], F32, kind="ExternalInput").ap()
    x_out = nc.dram_tensor("x_out", [T, D_MODEL], F32, kind="ExternalOutput").ap()
    w = {k: nc.dram_tensor(k, list(shp), F32, kind="ExternalInput").ap()
         for k, shp in [("Wq", (1024, 1024)), ("Wk", (1024, 1024)),
                        ("Wv", (1024, 1024)), ("bq", (1024,)), ("bk", (1024,)),
                        ("bv", (1024,))]}
    with ExitStack() as es:
        prog = Prog(nc)
        env = Env(nc, prog, es)
        sems = {e: es.enter_context(nc.semaphore(f"s_{e}")) for e in ENGINES}
        dma_sems = {q: es.enter_context(nc.semaphore(f"d_{q}")) for q in QUEUES}
        env.tril_dram = nc.dram_tensor("c_tril", [128, 128], F32,
                                       kind="ExternalInput").ap()
        ident, ones_col = declare_const_inputs(nc, env)
        emit_l1_attn_b(env, cfg, x_in, x_out, w, ident, ones_col)
        prog.op("sync", lambda e: e.nop(),
                reads=[("xout", b, it) for b in range(B) for it in range(L // 128)])
        block = es.enter_context(nc.Block())
        prog.finalize_and_emit(block, sems, dma_sems)
    return nc


def emit_l0_attn_a(env, cfg, x_in, x_out, w, ident, ones_col):
    p = env.prog
    B, L = cfg["B"], cfg["L"]
    ntb = L // 128
    H = 16
    G = L // 16
    NPAT = 32
    xt = [env.sbuf(f"xt{i}", [128, D_MODEL], F32) for i in range(2)]
    ht = [env.sbuf(f"ht{i}", [128, D_MODEL], BF16) for i in range(2)]
    hTt = [env.sbuf(f"hTt{i}", [128, D_MODEL], BF16) for i in range(2)]
    stats = env.sbuf("stats", [128, 16], F32)
    scratch = [env.sbuf(f"lnscratch{i}", [128, D_MODEL], F32) for i in range(2)]
    xob = [env.sbuf(f"xo{i}", [128, D_MODEL], F32) for i in range(2)]

    wq = env.sbuf("aa_wq", [128, NCH * D_MODEL], BF16)
    wo = env.sbuf("aa_wo", [128, NCH * D_MODEL], BF16)
    wkv_small = env.sbuf("aa_wkv", [128, NCH * 2 * D_MODEL], BF16)
    for c in range(NCH):
        p.dma("gpsimd", wq[:, c * 1024:(c + 1) * 1024],
              w["Wq"][c * 128:(c + 1) * 128, :], writes=[("aa_wq", c)])
        p.dma("gpsimd", wo[:, c * 1024:(c + 1) * 1024],
              w["Wo"][c * 128:(c + 1) * 128, :], writes=[("aa_wo", c)])
        p.dma("gpsimd", wkv_small[:, c * 2048:c * 2048 + 1024],
              w["Wk"][c * 128:(c + 1) * 128, :], writes=[("aa_wk", c)])
        p.dma("gpsimd", wkv_small[:, c * 2048 + 1024:c * 2048 + 2048],
              w["Wv"][c * 128:(c + 1) * 128, :], writes=[("aa_wv", c)])
    bqc = env.sbuf("aa_bq", [128, NCH], F32)
    bkc = env.sbuf("aa_bk", [128, NCH], F32)
    p.dma("gpsimd", bqc[:], w["bq"].rearrange("(c p) -> p c", p=128),
          writes=["aa_bq"], allow_slow_non_contiguous=True)
    p.dma("gpsimd", bkc[:], w["bk"].rearrange("(c p) -> p c", p=128),
          writes=["aa_bk"], allow_slow_non_contiguous=True)
    bvr = env.sbuf("aa_bv", [1, D_MODEL], BF16)
    bor = env.sbuf("aa_bo", [1, D_MODEL], BF16)
    p.dma("gpsimd", bvr[:], w["bv"].rearrange("(o d) -> o d", o=1), writes=["aa_bv"])
    p.dma("gpsimd", bor[:], w["bo"].rearrange("(o d) -> o d", o=1), writes=["aa_bo"])
    kpT = env.sbuf("aa_kpT", [128, NCH * NPAT], BF16)
    for c in range(NCH):
        p.dma("gpsimd", kpT[:, c * NPAT:(c + 1) * NPAT],
              w["kpT"][c * 128:(c + 1) * 128, :], writes=[("aa_kpT", c)])

    pp = env.psum("pp0", [128, 512], F32)
    pq = env.psum("pp1", [128, 512], F32)

    keysT = env.sbuf("aa_keysT", [128, NCH * NPAT], BF16)
    vals = env.sbuf("aa_vals", [32, D_MODEL], BF16)
    for oc in range(NCH):
        pairs = [(wkv_small[:, dc * 2048 + oc * 128: dc * 2048 + oc * 128 + 128],
                  kpT[:, dc * NPAT:(dc + 1) * NPAT],
                  [("aa_wk", dc), ("aa_kpT", dc)]) for dc in range(NCH)]
        mm_chain(env, pq[:, 0:NPAT], "pp1", pairs, tag="keys")
        p.op("scalar", lambda s, oc=oc: s.activation(
            keysT[:, oc * NPAT:(oc + 1) * NPAT], pq[:, 0:NPAT], AF.Identity,
            bias=bkc[:, oc:oc + 1]),
            reads=["pp1", "aa_bk"], writes=[("aa_keysT", oc)])
    for half in range(2):
        pairs = [(keysT[:, dc * NPAT:(dc + 1) * NPAT],
                  wkv_small[:, dc * 2048 + 1024 + half * 512: dc * 2048 + 1536 + half * 512],
                  [("aa_keysT", dc), ("aa_wv", dc)]) for dc in range(NCH)]
        mm_chain(env, pp[0:NPAT, :], "pp0", pairs, stop_last=False, tag="vals")
        p.op("tensor", lambda t, half=half: t.matmul(
            pp[0:NPAT, :], ones_col[:, 0:NPAT],
            bvr[:, half * 512:half * 512 + 512], start=False, stop=True),
            reads=["onescol", "aa_bv"], writes=["pp0"])
        p.op("scalar", lambda s, half=half: s.activation(
            vals[:, half * 512:half * 512 + 512], pp[0:NPAT, :], AF.Copy),
            reads=["pp0"], writes=[("aa_vals", half)])

    qT = env.sbuf("qb_qT", [128, NCH * L], BF16)
    zsb = env.sbuf("aa_z", [128, ntb * NPAT], F32)
    sortb = env.sbuf("aa_sort", [128, ntb * NPAT], F32)
    cumA = env.sbuf("aa_cumA", [128, ntb * NPAT], F32)
    cumB = env.sbuf("aa_cumB", [128, ntb * NPAT], F32)
    wk1 = env.sbuf("aa_wk1", [128, NPAT], F32)
    wk2 = env.sbuf("aa_wk2", [128, NPAT], F32)
    m8 = env.sbuf("aa_m8", [128, 8], F32)
    taub = env.sbuf("aa_tau", [128, ntb], F32)
    Asb = env.sbuf("aa_A", [128, ntb * NPAT], BF16)
    ATsb = env.sbuf("aa_AT", [32, L], BF16)
    OTsb = env.sbuf("aa_OT", [128, NCH * G], BF16)
    rk = env.sbuf("aa_rk", [128, ntb * NPAT], F32)
    p.dma("sync", rk[:], env.rk_dram[0:128, 0:ntb * NPAT], writes=["rk"])

    SCALE = 1.0 / 8.0

    for b in range(B):
        for it in range(ntb):
            par = it % 2
            row0 = b * L + it * 128
            p.dma("sync", xt[par][:], x_in[row0:row0 + 128, :], writes=[(f"xt{par}",)])
            emit_ln_tile(env, xt[par], (f"xt{par}",), ht[par], (f"ht{par}",),
                         stats[:, 8 * par:8 * par + 8], f"st{par}",
                         scratch[par], f"lnscratch{par}", D_MODEL)
            emit_transpose_tile(env, ht[par], (f"ht{par}",), hTt[par], f"hT{par}",
                                ident, NCH)
            hkeys = [(f"hT{par}", g) for g in range(2)]
            for oc in range(NCH):
                pairs = [(wq[:, dc * 1024 + oc * 128: dc * 1024 + oc * 128 + 128],
                          hTt[par][:, dc * 128:(dc + 1) * 128],
                          [("aa_wq", dc)] + hkeys) for dc in range(NCH)]
                mm_chain(env, pq[:, 0:128], "pp1", pairs, tag="q")
                p.op("scalar", lambda s, oc=oc, it=it: s.activation(
                    qT[:, oc * L + it * 128: oc * L + it * 128 + 128],
                    pq[:, 0:128], AF.Identity, bias=bqc[:, oc:oc + 1]),
                    reads=["pp1", "aa_bq"], writes=[("qT", oc, it)])
        for h in range(H):
            oc, po = h // 2, 64 * (h % 2)
            for it in range(ntb):
                p.op("tensor", lambda t, oc=oc, po=po, it=it: t.matmul(
                    pq[:, 0:NPAT],
                    qT[po:po + 64, oc * L + it * 128: oc * L + it * 128 + 128],
                    keysT[po:po + 64, oc * NPAT:(oc + 1) * NPAT],
                    start=True, stop=True),
                    reads=[("qT", oc, it), ("aa_keysT", oc)], writes=["pp1"],
                    tag="sc_aa")
                p.op("scalar", lambda s, it=it: s.activation(
                    zsb[:, it * NPAT:(it + 1) * NPAT], pq[:, 0:NPAT], AF.Copy,
                    scale=SCALE),
                    reads=["pp1"], writes=[("z", it)])
            for it in range(ntb):
                zk = ("z", it)
                p.op("vector", lambda v, it=it: v.tensor_copy(
                    wk1[:], zsb[:, it * NPAT:(it + 1) * NPAT]),
                    reads=[zk], writes=["wk1"])
                for r4 in range(4):
                    src = wk1 if r4 % 2 == 0 else wk2
                    dst = wk2 if r4 % 2 == 0 else wk1
                    sk, dk = ("wk1", "wk2") if r4 % 2 == 0 else ("wk2", "wk1")
                    p.op("vector", lambda v, src=src, it=it, r4=r4: v.max(
                        m8[:], src[:]), reads=[sk], writes=["m8"])
                    p.op("vector", lambda v, it=it, r4=r4: v.tensor_copy(
                        sortb[:, it * NPAT + r4 * 8: it * NPAT + r4 * 8 + 8],
                        m8[:]), reads=["m8"], writes=[("sort", it, r4)])
                    if r4 < 3:
                        p.op("vector", lambda v, src=src, dst=dst: v.match_replace(
                            dst[:], m8[:], src[:], -1e30),
                            reads=["m8", sk], writes=[dk])
            sort_keys = [("sort", it, r4) for it in range(ntb) for r4 in range(4)]
            W = ntb * NPAT
            p.op("vector", lambda v: v.tensor_copy(cumA[:, 0:W], sortb[:, 0:W]),
                 reads=sort_keys, writes=["cumA"])
            bufs = [(cumA, "cumA"), (cumB, "cumB")]
            cur = 0
            for k in (1, 2, 4, 8, 16):
                src, sk = bufs[cur]
                dst, dk = bufs[1 - cur]
                sa = src[:].rearrange("p (g n) -> p g n", n=NPAT)
                da = dst[:].rearrange("p (g n) -> p g n", n=NPAT)
                p.op("vector", lambda v, da=da, sa=sa, k=k: v.tensor_copy(
                    da[:, :, 0:k], sa[:, :, 0:k]), reads=[sk], writes=[dk])
                p.op("vector", lambda v, da=da, sa=sa, k=k: v.tensor_tensor(
                    da[:, :, k:NPAT], sa[:, :, k:NPAT], sa[:, :, 0:NPAT - k],
                    ALU.add), reads=[sk], writes=[dk])
                cur = 1 - cur
            cum, cumk = bufs[cur]
            oth, othk = bufs[1 - cur]
            p.op("vector", lambda v: v.tensor_scalar(
                cum[:, 0:W], cum[:, 0:W], -1.0, None, ALU.add),
                reads=[cumk], writes=[cumk])
            p.op("vector", lambda v: v.tensor_tensor(
                cum[:, 0:W], cum[:, 0:W], rk[:, 0:W], ALU.mult),
                reads=[cumk, "rk"], writes=[cumk])
            p.op("vector", lambda v: v.tensor_tensor(
                oth[:, 0:W], cum[:, 0:W], sortb[:, 0:W], ALU.is_lt),
                reads=[cumk] + sort_keys, writes=[othk])
            p.op("vector", lambda v: v.tensor_tensor(
                sortb[:, 0:W], oth[:, 0:W], cum[:, 0:W], ALU.mult),
                reads=[othk, cumk], writes=sort_keys)
            p.op("vector", lambda v: v.tensor_scalar(
                oth[:, 0:W], oth[:, 0:W], 1.0, 1e30, ALU.subtract, ALU.mult),
                reads=[othk], writes=[othk])
            p.op("vector", lambda v: v.tensor_tensor(
                oth[:, 0:W], oth[:, 0:W], sortb[:, 0:W], ALU.add),
                reads=[othk] + sort_keys, writes=[othk])
            p.op("vector", lambda v: v.tensor_reduce(
                taub[:].rearrange("p (g o) -> p g o", o=1),
                oth[:, 0:W].rearrange("p (g n) -> p g n", n=NPAT),
                AX.X, ALU.max), reads=[othk], writes=["tau"])
            zg = zsb[:, 0:W].rearrange("p (g n) -> p g n", n=NPAT)
            Ag = Asb[:, 0:W].rearrange("p (g n) -> p g n", n=NPAT)
            tg = taub[:].rearrange("p (g o) -> p g o", o=1).broadcast_to(
                (128, ntb, NPAT))
            p.op("vector", lambda v: v.tensor_tensor(
                cumA[:, 0:W].rearrange("p (g n) -> p g n", n=NPAT), zg, tg,
                ALU.subtract),
                reads=["tau"] + [("z", it) for it in range(ntb)], writes=["cumA"])
            p.op("vector", lambda v: v.tensor_scalar(
                Asb[:, 0:W], cumA[:, 0:W], 0.0, None, ALU.max),
                reads=["cumA"], writes=["A"])
            for g4 in range(cdiv(ntb, 4)):
                its = range(g4 * 4, min(ntb, g4 * 4 + 4))
                for i, it in enumerate(its):
                    p.op("tensor", lambda t, it=it, i=i: t.transpose(
                        pq[:].bitcast(BF16)[0:32, i * 128:(i + 1) * 128],
                        Asb[:, it * NPAT:(it + 1) * NPAT], ident[:]),
                        reads=["A", "ident"], writes=["pp1"])
                n = len(its) * 128
                p.op("scalar", lambda s, g4=g4, n=n: s.activation(
                    ATsb[:, g4 * 512:g4 * 512 + n],
                    pq[:].bitcast(BF16)[0:32, 0:n], AF.Copy),
                    reads=["pp1"], writes=[("AT", g4)])
            at_keys = [("AT", g4) for g4 in range(cdiv(ntb, 4))]
            for m in range(16):
                colblk = m // 4
                coloff = (m % 4) * G
                pos = 64 * (m % 2)
                ats = bass.AP(ATsb[:].tensor, m, [[L, 32], [16, G]])
                p.op("tensor", lambda t, m=m, ats=ats, h=h: t.matmul(
                         pp[64 * (m % 2):64 * (m % 2) + 64,
                            ((m // 2) % 4) * G:((m // 2) % 4) * G + G],
                         vals[:, h * 64:h * 64 + 64], ats,
                         start=True, stop=True),
                     reads=at_keys + [("aa_vals", 0), ("aa_vals", 1)],
                     writes=["pp0"], tag="Omm")
                if m % 8 == 7:
                    blk = m // 8
                    p.op("scalar", lambda s, blk=blk: s.activation(
                        OTsb[:, blk * 4 * G:(blk + 1) * 4 * G],
                        pp[:, 0:4 * G], AF.Copy),
                        reads=["pp0"], writes=[("OT", blk)])
            row0 = b * L + h * G
            par = h % 2
            p.dma("sync", xt[par][0:G, :], x_in[row0:row0 + G, :],
                  writes=[(f"xt{par}",)])
            for half in range(2):
                pairs = []
                for dc in range(NCH):
                    pairs.append((OTsb[:, dc * G:(dc + 1) * G],
                                  wo[:, dc * 1024 + half * 512: dc * 1024 + half * 512 + 512],
                                  [("OT", dc // 4), ("aa_wo", dc)]))
                mm_chain(env, pq[0:G, :], "pp1", pairs, stop_last=False, tag="wo")
                p.op("tensor", lambda t, half=half: t.matmul(
                    pq[0:G, :], ones_col[:, 0:G],
                    bor[:, half * 512:half * 512 + 512], start=False, stop=True),
                    reads=["onescol", "aa_bo"], writes=["pp1"])
                p.op("vector", lambda v, half=half, par=par: v.tensor_tensor(
                    xob[par][0:G, half * 512:half * 512 + 512],
                    pq[0:G, :], xt[par][0:G, half * 512:half * 512 + 512],
                    ALU.add),
                    reads=["pp1", (f"xt{par}",)],
                    writes=[(f"xo{par}", half)])
            p.dma("scalar", x_out[row0:row0 + G, :], xob[par][0:G, :],
                  reads=[(f"xo{par}", 0), (f"xo{par}", 1)],
                  writes=[("xout", b, h)], tag="xoaa")
            if getattr(env, "debug_aa", None) is not None:
                dbg = env.debug_aa
                idx = b * 16 + h
                p.dma("scalar", dbg["z"][idx], zsb[:, 0:ntb * NPAT],
                      reads=[("z", it) for it in range(ntb)],
                      writes=[("dbgz", idx)])
                p.dma("scalar", dbg["A"][idx], Asb[:, 0:ntb * NPAT],
                      reads=["A"], writes=[("dbgA", idx)])
                p.dma("scalar", dbg["OT"][idx], OTsb[:, 0:NCH * G],
                      reads=[("OT", 0), ("OT", 1)], writes=[("dbgOT", idx)])


def build_l0_attn_a_program(B, L):
    nc = bass.Bass("TRN2", target_bir_lowering=False)
    cfg = {"B": B, "L": L, "T": B * L}
    T = B * L
    x_in = nc.dram_tensor("x_in", [T, D_MODEL], F32, kind="ExternalInput").ap()
    x_out = nc.dram_tensor("x_out", [T, D_MODEL], F32, kind="ExternalOutput").ap()
    w = {k: nc.dram_tensor(k, list(shp), F32, kind="ExternalInput").ap()
         for k, shp in [("Wq", (1024, 1024)), ("Wk", (1024, 1024)),
                        ("Wv", (1024, 1024)), ("Wo", (1024, 1024)),
                        ("bq", (1024,)), ("bk", (1024,)), ("bv", (1024,)),
                        ("bo", (1024,)), ("kpT", (1024, 32))]}
    with ExitStack() as es:
        prog = Prog(nc)
        env = Env(nc, prog, es)
        sems = {e: es.enter_context(nc.semaphore(f"s_{e}")) for e in ENGINES}
        dma_sems = {q: es.enter_context(nc.semaphore(f"d_{q}")) for q in QUEUES}
        env.rk_dram = nc.dram_tensor("c_rk", [128, 512], F32,
                                     kind="ExternalInput").ap()
        nb_dram = nc.dram_tensor("c_negbig", [128, 512], F32,
                                 kind="ExternalInput").ap()
        ident, ones_col = declare_const_inputs(nc, env)
        env.neg_big = env.sbuf("negbig", [128, 512], F32)
        env.prog.dma("sync", env.neg_big[:], nb_dram[:], writes=["negbig"])
        ntb = L // 128
        G = L // 16
        import os
        if os.environ.get("AA_DEBUG"):
            env.debug_aa = {
                "z": nc.dram_tensor("dbg_z", [B * 16, 128, ntb * 32], F32,
                                    kind="ExternalOutput").ap(),
                "A": nc.dram_tensor("dbg_A", [B * 16, 128, ntb * 32], BF16,
                                    kind="ExternalOutput").ap(),
                "OT": nc.dram_tensor("dbg_OT", [B * 16, 128, NCH * G], BF16,
                                     kind="ExternalOutput").ap(),
            }
        emit_l0_attn_a(env, cfg, x_in, x_out, w, ident, ones_col)
        prog.op("sync", lambda e: e.nop(),
                reads=[("xout", b, h) for b in range(B) for h in range(16)])
        block = es.enter_context(nc.Block())
        prog.finalize_and_emit(block, sems, dma_sems)
    return nc


def emit_rope_chunk(env, qT, oc, L, src_keys, dst_key, cosT, sinST, swp,
                    rr_psum, rr_ev, rr_a):
    p = env.prog
    src_keys = list(src_keys)
    for jc in range(L // 512):
        sl = slice(oc * L + jc * 512, oc * L + jc * 512 + 512)
        p.op("tensor", lambda t, sl=sl: t.matmul(
            rr_psum[:], swp[:], qT[:, sl], start=True, stop=True),
            reads=src_keys + ["swp"], writes=["pp1"], tag="rope_swap")
        p.op("scalar", lambda s: s.activation(rr_ev[:], rr_psum[:], AF.Copy),
             reads=["pp1"], writes=["rr_ev"])
        p.op("vector", lambda v, sl=sl, jc=jc: v.tensor_tensor(
            rr_a[:], qT[:, sl], cosT[:, jc * 512:jc * 512 + 512], ALU.mult),
            reads=src_keys + ["rope"], writes=["rr_a"])
        p.op("vector", lambda v, jc=jc: v.tensor_tensor(
            rr_ev[:], rr_ev[:], sinST[:, jc * 512:jc * 512 + 512], ALU.mult),
            reads=["rr_ev", "rope"], writes=["rr_ev"])
        p.op("vector", lambda v, sl=sl: v.tensor_tensor(
            qT[:, sl], rr_a[:], rr_ev[:], ALU.add),
            reads=["rr_a", "rr_ev"], writes=[dst_key])


def emit_l0_attn_b(env, cfg, x_in, x_out, w, ident, ones_col):
    p = env.prog
    B, L = cfg["B"], cfg["L"]
    ntb = L // 128
    H, KVH, E = 16, 4, 64
    SCALE = 64.0 ** -0.5
    xt = [env.sbuf(f"xt{i}", [128, D_MODEL], F32) for i in range(2)]
    ht = [env.sbuf(f"ht{i}", [128, D_MODEL], BF16) for i in range(2)]
    hTt = [env.sbuf(f"hTt{i}", [128, D_MODEL], BF16) for i in range(2)]
    xob = [env.sbuf(f"xo{i}", [128, D_MODEL], F32) for i in range(2)]
    stats = env.sbuf("stats", [128, 16], F32)
    scratch = [env.sbuf(f"lnscratch{i}", [128, D_MODEL], F32) for i in range(2)]

    wq = env.sbuf("ab_wq", [128, NCH * D_MODEL], BF16)
    wkv = env.sbuf("ab_wkv", [128, NCH * 512], BF16)
    wo = env.sbuf("ab_wo", [128, NCH * D_MODEL], BF16)
    bor = env.sbuf("ab_bo", [1, D_MODEL], BF16)
    for c in range(NCH):
        p.dma("gpsimd", wq[:, c * 1024:(c + 1) * 1024],
              w["Wq"][c * 128:(c + 1) * 128, :], writes=[("ab_wq", c)])
        p.dma("gpsimd", wkv[:, c * 512:(c + 1) * 512],
              w["Wkv"][c * 128:(c + 1) * 128, :], writes=[("ab_wkv", c)])
        p.dma("gpsimd", wo[:, c * 1024:(c + 1) * 1024],
              w["Wo"][c * 128:(c + 1) * 128, :], writes=[("ab_wo", c)])
    p.dma("gpsimd", bor[:], w["bo"].rearrange("(o d) -> o d", o=1), writes=["ab_bo"])

    cosT = env.sbuf("rope_cos", [128, L], F32)
    sinST = env.sbuf("rope_sin", [128, L], F32)
    p.dma("sync", cosT[:], env.rope_cos_dram[:, 0:L], writes=["rope"])
    p.dma("sync", sinST[:], env.rope_sin_dram[:, 0:L], writes=["rope"])
    swp = env.sbuf("rope_swap", [128, 128], BF16)
    p.dma("sync", swp[:], env.rope_swap_dram[:], writes=["swp"])
    rr_ev = env.sbuf("rope_ev", [128, 512], BF16)
    rr_a = env.sbuf("rope_a", [128, 512], BF16)

    qT = env.sbuf("qb_qT", [128, NCH * L], BF16)
    kvT = env.sbuf("ab_kvT", [128, 4 * L], BF16)
    vtok = env.sbuf("ab_vtok", [128, KVH * ntb * 64], BF16)
    esb = [env.sbuf(f"ab_e{i}", [128, L], BF16) for i in range(4)]
    etsb = [env.sbuf(f"ab_et{i}", [128, 512], BF16) for i in range(2)]
    zcols = env.sbuf("ab_z", [128, 16], F32)
    rzb = env.sbuf("ab_rz", [128, 512], F32)
    spp = env.psum("spp", [128, 2048], F32)
    tpp = env.psum("tpp", [128, 512], BF16)
    pvp = env.psum("pvp", [128, 512], F32)
    pq = env.psum("pp1", [128, 512], F32)
    zrow_dram = env.dram_scratch("zrow", [B * H * L], F32)

    for b in range(B):
        for it in range(ntb):
            par = it % 2
            row0 = b * L + it * 128
            p.dma("sync", xt[par][:], x_in[row0:row0 + 128, :], writes=[(f"xt{par}",)])
            emit_ln_tile(env, xt[par], (f"xt{par}",), ht[par], (f"ht{par}",),
                         stats[:, 8 * par:8 * par + 8], f"st{par}",
                         scratch[par], f"lnscratch{par}", D_MODEL)
            emit_transpose_tile(env, ht[par], (f"ht{par}",), hTt[par], f"hT{par}",
                                ident, NCH)
            hkeys = [(f"hT{par}", g) for g in range(2)]
            for oc in range(NCH):
                pairs = [(wq[:, dc * 1024 + oc * 128: dc * 1024 + oc * 128 + 128],
                          hTt[par][:, dc * 128:(dc + 1) * 128],
                          [("ab_wq", dc)] + hkeys) for dc in range(NCH)]
                mm_chain(env, pq[:, 0:128], "pp1", pairs, tag="qp")
                p.op("scalar", lambda s, oc=oc, it=it: s.activation(
                    qT[:, oc * L + it * 128: oc * L + it * 128 + 128],
                    pq[:, 0:128], AF.Copy),
                    reads=["pp1"], writes=[("qT", oc, it)])
            for oc in range(4):
                pairs = [(wkv[:, dc * 512 + oc * 128: dc * 512 + oc * 128 + 128],
                          hTt[par][:, dc * 128:(dc + 1) * 128],
                          [("ab_wkv", dc)] + hkeys) for dc in range(NCH)]
                mm_chain(env, pq[:, 0:128], "pp1", pairs, tag="kvp")
                p.op("scalar", lambda s, oc=oc, it=it: s.activation(
                    kvT[:, oc * L + it * 128: oc * L + it * 128 + 128],
                    pq[:, 0:128], AF.Copy),
                    reads=["pp1"], writes=[("kvT", oc, it)])
        qkeys = [("qT", oc, it) for oc in range(NCH) for it in range(ntb)]
        kvkeys = [("kvT", oc, it) for oc in range(4) for it in range(ntb)]
        for oc in range(NCH):
            emit_rope_chunk(env, qT, oc, L,
                            [("qT", oc, it) for it in range(ntb)],
                            ("qTr", oc), cosT, sinST, swp, pq, rr_ev, rr_a)
        for oc in range(2):
            emit_rope_chunk(env, kvT, oc, L,
                            [("kvT", oc, it) for it in range(ntb)],
                            ("kTr", oc), cosT, sinST, swp, pq, rr_ev, rr_a)
        for kh in range(KVH):
            oc = 2 + kh // 2
            po = 64 * (kh % 2)
            for jt in range(ntb):
                p.op("tensor", lambda t, oc=oc, po=po, jt=jt, kh=kh: t.transpose(
                    tpp[:, 0:64], kvT[po:po + 64, oc * L + jt * 128: oc * L + jt * 128 + 128],
                    ident[po:po + 64, po:po + 64]),
                    reads=kvkeys + ["ident"], writes=["tpp"])
                p.op("scalar", lambda s, kh=kh, jt=jt: s.activation(
                    vtok[:, (kh * ntb + jt) * 64:(kh * ntb + jt) * 64 + 64],
                    tpp[:, 0:64], AF.Copy),
                    reads=["tpp"], writes=[("vtok", kh, jt)])
        for h in range(H):
            oc, po = ab_head_pos(h)
            kh = h // 4
            koc, kpo = kh // 2, 64 * (kh % 2)
            for ig in range(ntb // 4):
                for i4 in range(4):
                    it = ig * 4 + i4
                    for jc in range(L // 512):
                        pairs = [(qT[po:po + 64, oc * L + it * 128: oc * L + it * 128 + 128],
                                  kvT[kpo:kpo + 64, koc * L + jc * 512: koc * L + jc * 512 + 512],
                                  [("qTr", oc), ("kTr", koc)])]
                        mm_chain(env, spp[:, jc * 512:(jc + 1) * 512],
                                 ("spp", jc), pairs, tag="scb")
                    s_m = stats[:, 0:1]
                    s_negm = stats[:, 1:2]
                    spkeys = [("spp", jc) for jc in range(L // 512)]
                    p.op("vector", lambda v: v.tensor_reduce(
                        s_m, spp[:, 0:L], AX.X, ALU.max),
                        reads=spkeys, writes=[("stats", 0)])
                    p.op("vector", lambda v: v.tensor_scalar(
                        s_negm, s_m, -SCALE, None, ALU.mult),
                        reads=[("stats", 0)], writes=[("stats", 1)])
                    p.op("scalar", lambda s, i4=i4, it=it: s.activation(
                        esb[i4][:, 0:L], spp[:, 0:L], AF.Exp, scale=SCALE,
                        bias=s_negm, accum_out=zcols[:, it:it + 1]),
                        reads=spkeys + [("stats", 1)],
                        writes=[("e", i4), ("zc", it)])
                zoff = (b * H + h) * L + ig * 512
                p.dma("gpsimd",
                      bass.AP(zrow_dram.ap().tensor, zoff, [[1, 128], [128, 4]]),
                      zcols[:, ig * 4:ig * 4 + 4],
                      reads=[("zc", ig * 4 + i) for i in range(4)],
                      writes=[("zrow", ig)], allow_slow_non_contiguous=True)
                p.dma("gpsimd", rzb[po:po + 64, 0:512],
                      bass.AP(zrow_dram.ap().tensor, zoff, [[0, 64], [1, 512]]),
                      reads=[("zrow", ig)], writes=["rzb"])
                p.op("vector", lambda v, po=po: v.reciprocal(
                    rzb[po:po + 64, 0:512], rzb[po:po + 64, 0:512]),
                     reads=["rzb"], writes=["rzb"])
                for jt in range(ntb):
                    for i4 in range(4):
                        p.op("tensor", lambda t, i4=i4, jt=jt: t.transpose(
                            tpp[:].bitcast(BF16)[:, i4 * 128:(i4 + 1) * 128],
                            esb[i4][:, jt * 128:(jt + 1) * 128], ident[:]),
                            reads=[("e", i4), "ident"], writes=["tpp"])
                    eb = jt % 2
                    p.op("scalar", lambda s, eb=eb: s.activation(
                        etsb[eb][:], tpp[:].bitcast(BF16)[:, 0:512], AF.Copy),
                        reads=["tpp"], writes=[("et", eb)])
                    p.op("tensor", lambda t, kh=kh, jt=jt, eb=eb, po=po: t.matmul(
                        pvp[po:po + 64, :],
                        vtok[:, (kh * ntb + jt) * 64:(kh * ntb + jt) * 64 + 64],
                        etsb[eb][:], start=(jt == 0), stop=(jt == ntb - 1)),
                        reads=[("vtok", kh, jt), ("et", eb)], writes=["pvp"],
                        tag="pv")
                p.op("vector", lambda v, oc=oc, po=po, ig=ig: v.tensor_tensor(
                    qT[po:po + 64, oc * L + ig * 512: oc * L + ig * 512 + 512],
                    pvp[po:po + 64, :], rzb[po:po + 64, 0:512], ALU.mult),
                    reads=["pvp", "rzb"], writes=[("attnT", h, ig)])
        for it in range(ntb):
            par = it % 2
            row0 = b * L + it * 128
            p.dma("sync", xt[par][:], x_in[row0:row0 + 128, :], writes=[(f"xt{par}",)])
            akeys = [("attnT", hh, it // 4) for hh in range(H)]
            for half in range(2):
                pairs = []
                for dc in range(NCH):
                    pairs.append((qT[:, dc * L + it * 128: dc * L + it * 128 + 128],
                                  wo[:, dc * 1024 + half * 512: dc * 1024 + half * 512 + 512],
                                  akeys + [("ab_wo", dc)]))
                mm_chain(env, pq[:], "pp1", pairs, stop_last=False, tag="wob")
                p.op("tensor", lambda t, half=half: t.matmul(
                    pq[:], ones_col[:], bor[:, half * 512:half * 512 + 512],
                    start=False, stop=True),
                    reads=["onescol", "ab_bo"], writes=["pp1"])
                p.op("vector", lambda v, half=half, par=par: v.tensor_tensor(
                    xob[par][:, half * 512:half * 512 + 512], pq[:],
                    xt[par][:, half * 512:half * 512 + 512], ALU.add),
                    reads=["pp1", (f"xt{par}",)], writes=[(f"xo{par}", half)])
            p.dma("scalar", x_out[row0:row0 + 128, :], xob[par][:],
                  reads=[(f"xo{par}", 0), (f"xo{par}", 1)],
                  writes=[("xout", b, it)], tag="xob")


def build_l0_attn_b_program(B, L):
    nc = bass.Bass("TRN2", target_bir_lowering=False)
    cfg = {"B": B, "L": L, "T": B * L}
    T = B * L
    x_in = nc.dram_tensor("x_in", [T, D_MODEL], F32, kind="ExternalInput").ap()
    x_out = nc.dram_tensor("x_out", [T, D_MODEL], F32, kind="ExternalOutput").ap()
    w = {k: nc.dram_tensor(k, list(shp), F32, kind="ExternalInput").ap()
         for k, shp in [("Wq", (1024, 1024)), ("Wkv", (1024, 512)),
                        ("Wo", (1024, 1024)), ("bo", (1024,))]}
    with ExitStack() as es:
        prog = Prog(nc)
        env = Env(nc, prog, es)
        sems = {e: es.enter_context(nc.semaphore(f"s_{e}")) for e in ENGINES}
        dma_sems = {q: es.enter_context(nc.semaphore(f"d_{q}")) for q in QUEUES}
        env.rope_cos_dram = nc.dram_tensor("c_ropecos", [128, L], F32,
                                           kind="ExternalInput").ap()
        env.rope_sin_dram = nc.dram_tensor("c_ropesin", [128, L], F32,
                                           kind="ExternalInput").ap()
        env.rope_swap_dram = nc.dram_tensor("c_ropeswap", [128, 128], BF16,
                                            kind="ExternalInput").ap()
        ident, ones_col = declare_const_inputs(nc, env)
        emit_l0_attn_b(env, cfg, x_in, x_out, w, ident, ones_col)
        prog.op("sync", lambda e: e.nop(),
                reads=[("xout", b, it) for b in range(B) for it in range(L // 128)])
        block = es.enter_context(nc.Block())
        prog.finalize_and_emit(block, sems, dma_sems)
    return nc


def attn_b_head_perm():
    perm = np.zeros(1024, dtype=np.int64)
    for h in range(16):
        hc = h if h < 4 else (h - 4 if h < 8 else (h - 4 if h < 12 else h - 8))
        off = 64 * ((h // 4) % 2)
        for d in range(64):
            perm[hc * 128 + off + d] = h * 64 + d
    return perm


def ab_head_pos(h):
    hc = h if h < 4 else (h - 4 if h < 8 else (h - 4 if h < 12 else h - 8))
    return hc, 64 * ((h // 4) % 2)


def rope_tables(L, d=64):
    inv_freq = 1.0 / 10000.0 ** (np.arange(0, d, 2) / d)
    freqs = np.arange(L)[:, None] * inv_freq[None, :]
    freqs = np.concatenate([freqs, freqs], axis=1)
    cos = np.cos(freqs).T.astype(np.float32)
    sin = np.sin(freqs).T.astype(np.float32)
    sinS = sin.copy()
    sinS[0:32] = -sinS[0:32]
    cos128 = np.concatenate([cos, cos], axis=0)
    sin128 = np.concatenate([sinS, sinS], axis=0)
    return np.ascontiguousarray(cos128), np.ascontiguousarray(sin128)


def rope_swap_matrix():
    S = np.zeros((128, 128), np.float32)
    for blk in range(2):
        for j in range(32):
            S[blk * 64 + 32 + j, blk * 64 + j] = 1.0
            S[blk * 64 + j, blk * 64 + 32 + j] = 1.0
    return S.astype(BF)


def emit_mamba(env, cfg, x_in, x_out, w, ident, ones_col):
    p = env.prog
    B, L = cfg["B"], cfg["L"]
    ntb = L // 128
    ntc = L // 512
    DI, NST = 2048, 16
    NDT = DI // 128

    xt = [env.sbuf(f"xt{i}", [128, D_MODEL], F32) for i in range(2)]
    ut = [env.sbuf(f"ut{i}", [128, D_MODEL], F32) for i in range(2)]
    ht = [env.sbuf(f"ht{i}", [128, D_MODEL], BF16) for i in range(2)]
    hTt = [env.sbuf(f"hTt{i}", [128, D_MODEL], BF16) for i in range(2)]
    xob = [env.sbuf(f"xo{i}", [128, D_MODEL], F32) for i in range(2)]
    stats = env.sbuf("stats", [128, 16], F32)
    stats2 = env.sbuf("stats2", [128, 16], F32)
    scratch = [env.sbuf(f"lnscratch{i}", [128, D_MODEL], F32) for i in range(2)]

    w_ing_dram = env.dram_scratch("w_ing", [1024, 4096], BF16)
    wout_dram = env.dram_scratch("w_outb", [2048, 1024], BF16)
    bincol_dram = env.dram_scratch("ma_bincol", [4096], F32)
    wtmp32 = env.sbuf("ma_w32", [128, 1024], F32)
    wtmpb = env.sbuf("ma_wb", [128, 1024], BF16)
    gcol = env.sbuf("ma_gcol", [128, NCH], F32)
    bcol = env.sbuf("ma_bcol", [128, NCH], BF16)
    p.dma("gpsimd", gcol[:], w["ln_g"].rearrange("(c p) -> p c", p=128),
          writes=["ma_g"], allow_slow_non_contiguous=True)
    p.dma("gpsimd", bcol[:], w["ln_b"].rearrange("(c p) -> p c", p=128),
          writes=["ma_b"], allow_slow_non_contiguous=True)
    binrow = env.sbuf("ma_binrow", [1, 4096], F32)
    pbias = env.psum("pp0", [128, 512], F32)
    pq = env.psum("pp1", [128, 512], F32)
    for q4 in range(4):
        for dc in range(NCH):
            p.dma("sync", wtmp32[:], w["W_in"][dc * 128:(dc + 1) * 128,
                                               q4 * 1024:(q4 + 1) * 1024],
                  writes=["ma_w32"])
            p.op("vector", lambda v: v.tensor_copy(wtmpb[:], wtmp32[:]),
                 reads=["ma_w32"], writes=["ma_wb"])
            for hh in range(2):
                p.op("tensor", lambda t, dc=dc, hh=hh: t.matmul(
                    pbias[0:1, hh * 0:512] if False else pbias[0:1, :],
                    bcol[:, dc:dc + 1], wtmpb[:, hh * 512:hh * 512 + 512],
                    start=(dc == 0 and hh == 0) if False else (dc == 0),
                    stop=(dc == NCH - 1)),
                    reads=["ma_wb", "ma_b"], writes=[("ppb", hh)], tag="mbias") \
                    if False else None
            for hh in range(2):
                p.op("tensor", lambda t, dc=dc, hh=hh: t.matmul(
                    pbias[0:1, 0:512] if hh == 0 else pq[0:1, 0:512],
                    bcol[:, dc:dc + 1], wtmpb[:, hh * 512:hh * 512 + 512],
                    start=(dc == 0), stop=(dc == NCH - 1)),
                    reads=["ma_wb", "ma_b"],
                    writes=["pp0" if hh == 0 else "pp1"], tag="mbias")
            p.op("vector", lambda v, dc=dc: v.tensor_scalar(
                wtmpb[:], wtmp32[:], gcol[:, dc:dc + 1], None, ALU.mult),
                reads=["ma_w32", "ma_g", "pp0", "pp1"], writes=["ma_wb"])
            p.dma("scalar", w_ing_dram.ap()[dc * 128:(dc + 1) * 128,
                                            q4 * 1024:(q4 + 1) * 1024],
                  wtmpb[:], reads=["ma_wb"], writes=[("w_ing", dc, q4)])
        for hh in range(2):
            p.op("scalar", lambda s, q4=q4, hh=hh: s.activation(
                binrow[:, q4 * 1024 + hh * 512: q4 * 1024 + hh * 512 + 512],
                (pbias if hh == 0 else pq)[0:1, 0:512], AF.Copy),
                reads=["pp0" if hh == 0 else "pp1"],
                writes=[("ma_binrow", q4, hh)])
    p.dma("scalar", bincol_dram.ap().rearrange("(o d) -> o d", o=1), binrow[:],
          reads=[("ma_binrow", q4, hh) for q4 in range(4) for hh in range(2)],
          writes=["bincol_d"])
    bincols = env.sbuf("ma_bincols", [128, 32], F32)
    p.dma("gpsimd", bincols[:], bincol_dram.ap().rearrange("(c p) -> p c", p=128),
          reads=["bincol_d"], writes=["ma_bincols"],
          allow_slow_non_contiguous=True)
    for dc in range(16):
        p.dma("sync", wtmp32[:], w["W_out"][dc * 128:(dc + 1) * 128, :],
              writes=["ma_w32"])
        p.op("vector", lambda v: v.tensor_copy(wtmpb[:], wtmp32[:]),
             reads=["ma_w32"], writes=["ma_wb"])
        p.dma("scalar", wout_dram.ap()[dc * 128:(dc + 1) * 128, :], wtmpb[:],
              reads=["ma_wb"], writes=[("woutd", dc)])

    A_sb = env.sbuf("ma_A", [128, NDT * NST], F32)
    for dt_ in range(NDT):
        p.dma("sync", A_sb[:, dt_ * NST:(dt_ + 1) * NST],
              w["A_log"][dt_ * 128:(dt_ + 1) * 128, :], writes=[("ma_Araw", dt_)])
    p.op("scalar", lambda s: s.activation(A_sb[:], A_sb[:], AF.Exp),
         reads=[("ma_Araw", dt_) for dt_ in range(NDT)], writes=["ma_Aexp"])
    p.op("vector", lambda v: v.tensor_scalar(A_sb[:], A_sb[:], -1.0, None,
                                             ALU.mult),
         reads=["ma_Aexp"], writes=["ma_A"])
    convw = env.sbuf("ma_convw", [128, NDT * 4], F32)
    for dt_ in range(NDT):
        p.dma("sync", convw[:, dt_ * 4:(dt_ + 1) * 4],
              w["conv_w"][dt_ * 128:(dt_ + 1) * 128, :], writes=["ma_convw"])
    convb = env.sbuf("ma_convb", [128, NDT], F32)
    p.dma("gpsimd", convb[:], w["conv_b"].rearrange("(c p) -> p c", p=128),
          writes=["ma_convb"], allow_slow_non_contiguous=True)
    bdtc = env.sbuf("ma_bdt", [128, NDT], F32)
    p.dma("gpsimd", bdtc[:], w["b_dt"].rearrange("(c p) -> p c", p=128),
          writes=["ma_bdt"], allow_slow_non_contiguous=True)
    dcol = env.sbuf("ma_D", [128, NDT], F32)
    p.dma("gpsimd", dcol[:], w["D_param"].rearrange("(c p) -> p c", p=128),
          writes=["ma_D"], allow_slow_non_contiguous=True)
    wxp = env.sbuf("ma_wxp", [128, NDT * 96], BF16)
    for dt_ in range(NDT):
        p.dma("gpsimd", wxp[:, dt_ * 96:(dt_ + 1) * 96],
              w["W_xproj"][dt_ * 128:(dt_ + 1) * 128, :], writes=["ma_wxp"])
    wdt = env.sbuf("ma_wdt", [64, DI], BF16)
    p.dma("gpsimd", wdt[:], w["W_dt"][:], writes=["ma_wdt"])

    hTc = env.sbuf("ma_hTc", [128, NCH * 512], BF16)
    woutb = env.sbuf("ma_wout", [128, 16 * 1024], BF16)
    for dc in range(16):
        p.dma("sync", woutb[:, dc * 1024:(dc + 1) * 1024],
              wout_dram.ap()[dc * 128:(dc + 1) * 128, :],
              reads=[("woutd", dc)], writes=[("wout", dc)])
    bcb = env.sbuf("ma_bcb", [128, 2 * NST * 512], BF16)
    ybb = env.sbuf("ma_ybb", [128, NDT * 512], BF16)
    dbcT = env.sbuf("ma_dbc", [128, L], BF16)
    xcpad = env.sbuf("ma_xcpad", [128, L + 8], BF16)
    convacc = env.sbuf("ma_convacc", [128, L], F32)
    deltac = env.sbuf("ma_delta", [128, 512], F32)
    wchunk = env.sbuf("ma_wchunk", [128, 512], BF16)
    achunk = env.sbuf("ma_achunk", [128, 512], F32)
    schunk = env.sbuf("ma_schunk", [128, 512], BF16)
    tmpc = env.sbuf("ma_tmpc", [128, 512], BF16)
    ybuf = env.sbuf("ma_ybuf", [128, 512], F32)
    carry = env.sbuf("ma_carry", [128, NDT * NST], F32)
    zchunk = env.sbuf("ma_zchunk", [128, 512], BF16)
    sz32 = env.sbuf("ma_sz", [128, 512], F32)
    sp_abs = env.sbuf("ma_spabs", [128, 512], F32)
    sp_ln = env.sbuf("ma_spln", [128, 512], F32)
    sp_r = env.sbuf("ma_spr", [128, 512], F32)
    wcol = env.sbuf("ma_wcol", [128, NCH * 128], BF16)
    zev = env.sbuf("ma_zev", [128, 512], BF16)
    xcchunk = env.sbuf("ma_xcc", [128, 512], BF16)

    hT_dram = env.dram_scratch("ma_hT", [1024, L], BF16)
    xinT_dram = env.dram_scratch("ma_xinT", [DI, L], BF16)
    xc_dram = env.dram_scratch("ma_xc", [DI, L], BF16)
    zT_dram = env.dram_scratch("ma_zT", [DI, L], BF16)
    xu_dram = env.dram_scratch("ma_xu", [cfg["T"], D_MODEL], F32)
    bc_dram = env.dram_scratch("ma_bc", [32, L], BF16)

    for b in range(B):
        for it in range(ntb):
            par = it % 2
            row0 = b * L + it * 128
            p.dma("sync", xt[par][:], x_in[row0:row0 + 128, :], writes=[(f"xt{par}",)])
            emit_ln_tile(env, xt[par], (f"xt{par}",), ut[par], (f"ut{par}",),
                         stats[:, 8 * par:8 * par + 8], f"st{par}",
                         scratch[par], f"lnscratch{par}", D_MODEL)
            p.op("vector", lambda v, par=par: v.tensor_tensor(
                xob[par][:], xt[par][:], ut[par][:], ALU.add),
                reads=[(f"xt{par}",), (f"ut{par}",)], writes=[(f"xo{par}", 0)])
            p.dma("scalar", xu_dram.ap()[row0:row0 + 128, :], xob[par][:],
                  reads=[(f"xo{par}", 0)], writes=[("xu", b, it)])
            emit_ln_tile(env, ut[par], (f"ut{par}",), ht[par], (f"ht{par}",),
                         stats2[:, 8 * par:8 * par + 8], f"s2t{par}",
                         scratch[par], f"lnscratch{par}", D_MODEL)
            emit_transpose_tile(env, ht[par], (f"ht{par}",), hTt[par], f"hT{par}",
                                ident, NCH)
            p.dma("scalar",
                  hT_dram.ap().rearrange("(c p) t -> p c t", p=128)[
                      :, :, it * 128:(it + 1) * 128],
                  hTt[par][:].rearrange("p (c n) -> p c n", n=128),
                  reads=[(f"hT{par}", g) for g in range(2)],
                  writes=[("hTd", it)])
        for tc in range(ntc):
            p.dma("sync", hTc[:],
                  hT_dram.ap().rearrange("(c p) t -> p c t", p=128)[
                      :, :, tc * 512:(tc + 1) * 512],
                  reads=[("hTd", it) for it in range(ntb)], writes=["hTc"])
            for ot in range(32):
                q4 = ot // 8
                for dc in range(NCH):
                    p.dma("sync", wcol[:, dc * 128:(dc + 1) * 128],
                          w_ing_dram.ap()[dc * 128:(dc + 1) * 128,
                                          ot * 128:(ot + 1) * 128],
                          reads=[("w_ing", dc, q4)], writes=[("wcol", dc)])
                pairs = [(wcol[:, dc * 128:(dc + 1) * 128],
                          hTc[:, dc * 512:(dc + 1) * 512],
                          [("wcol", dc), "hTc"]) for dc in range(NCH)]
                mm_chain(env, pq[:], "pp1", pairs, tag="xz")
                p.op("scalar", lambda s, ot=ot: s.activation(
                    zev[:], pq[:], AF.Identity, bias=bincols[:, ot:ot + 1]),
                    reads=["pp1", "ma_bincols"], writes=["zev"])
                dst = (xinT_dram if ot < 16 else zT_dram).ap()
                r0 = (ot % 16) * 128
                p.dma("scalar", dst[r0:r0 + 128, tc * 512:tc * 512 + 512],
                      zev[:], reads=["zev"],
                      writes=[("xinT" if ot < 16 else "zT", b, ot % 16, tc)])
        for dt_ in range(NDT):
            p.op("vector", lambda v: v.memset(xcpad[:, 0:4], 0.0),
                 writes=[("xcpad", "z")])
            p.dma("sync", xcpad[:, 4:4 + L], xinT_dram.ap()[dt_ * 128:(dt_ + 1) * 128, :],
                  reads=[("xinT", b, dt_, tc) for tc in range(ntc)],
                  writes=[("xcpad", "d")])
            p.op("vector", lambda v, dt_=dt_: v.tensor_scalar(
                convacc[:], xcpad[:, 1:1 + L], convw[:, dt_ * 4:dt_ * 4 + 1],
                convb[:, dt_:dt_ + 1], ALU.mult, ALU.add),
                reads=[("xcpad", "z"), ("xcpad", "d"), "ma_convw", "ma_convb"],
                writes=["convacc"])
            for k in range(1, 4):
                p.op("vector", lambda v, dt_=dt_, k=k: v.scalar_tensor_tensor(
                    convacc[:], xcpad[:, 1 + k:1 + k + L],
                    convw[:, dt_ * 4 + k:dt_ * 4 + k + 1], convacc[:],
                    ALU.mult, ALU.add),
                    reads=[("xcpad", "z"), ("xcpad", "d"), "ma_convw", "convacc"],
                    writes=["convacc"])
            p.op("scalar", lambda s: s.activation(xcpad[:, 4:4 + L], convacc[:],
                                                  AF.Silu),
                 reads=["convacc"], writes=[("xcpad", "d")])
            p.dma("scalar", xc_dram.ap()[dt_ * 128:(dt_ + 1) * 128, :],
                  xcpad[:, 4:4 + L], reads=[("xcpad", "d")],
                  writes=[("xc", b, dt_)])
        for tc in range(ntc):
            p.dma("sync", xcchunk[:], xc_dram.ap()[0:128, tc * 512:tc * 512 + 512],
                  reads=[("xc", b, 0)], writes=["xcc"]) if False else None
            pairs = []
            for dt_ in range(NDT):
                pass
            for dt_ in range(NDT):
                p.dma("sync", xcchunk[:], xc_dram.ap()[dt_ * 128:(dt_ + 1) * 128,
                                                       tc * 512:tc * 512 + 512],
                      reads=[("xc", b, dt_)], writes=["xcc"])
                p.op("tensor", lambda t, dt_=dt_: t.matmul(
                    pq[0:96, :], wxp[:, dt_ * 96:(dt_ + 1) * 96], xcchunk[:],
                    start=(dt_ == 0), stop=(dt_ == NDT - 1)),
                    reads=["ma_wxp", "xcc"], writes=["pp1"], tag="dbc")
            p.op("scalar", lambda s, tc=tc: s.activation(
                dbcT[0:96, tc * 512:tc * 512 + 512], pq[0:96, :], AF.Copy),
                reads=["pp1"], writes=[("dbcT", tc)])
        p.dma("scalar", bc_dram.ap()[:, :], dbcT[64:96, 0:L],
              reads=[("dbcT", tc) for tc in range(ntc)], writes=[("bcd", b)])
        p.op("vector", lambda v: v.memset(carry[:], 0.0), writes=["carry"])
        for tc in range(ntc):
            for n in range(2 * NST):
                p.dma("gpsimd", bcb[:, n * 512:(n + 1) * 512],
                      bass.AP(bc_dram.ap().tensor, n * L + tc * 512,
                              [[0, 128], [1, 512]]),
                      reads=[("bcd", b)], writes=[("bcb", n)])
            for dt_ in range(NDT):
                p.op("tensor", lambda t, dt_=dt_, tc=tc: t.matmul(
                    pq[:], wdt[:, dt_ * 128:(dt_ + 1) * 128],
                    dbcT[0:64, tc * 512:tc * 512 + 512],
                    start=True, stop=True),
                    reads=["ma_wdt", ("dbcT", tc)], writes=["pp1"], tag="dtm")
                p.op("vector", lambda v, dt_=dt_: v.tensor_scalar(
                    sp_r[:], pq[:], bdtc[:, dt_:dt_ + 1], 0.0,
                    ALU.add, ALU.max),
                    reads=["pp1", "ma_bdt"], writes=["spr"])
                p.op("vector", lambda v, dt_=dt_: v.tensor_scalar(
                    sp_abs[:], pq[:], bdtc[:, dt_:dt_ + 1], None, ALU.add),
                    reads=["pp1", "ma_bdt"], writes=["spabs"])
                p.op("scalar", lambda s: s.activation(sp_abs[:], sp_abs[:], AF.Abs),
                     reads=["spabs"], writes=["spabs"])
                p.op("scalar", lambda s: s.activation(sp_ln[:], sp_abs[:], AF.Exp,
                                                      scale=-1.0),
                     reads=["spabs"], writes=["spln"])
                p.op("scalar", lambda s: s.activation(sp_ln[:], sp_ln[:], AF.Ln,
                                                      bias=1.0),
                     reads=["spln"], writes=["spln"])
                p.op("vector", lambda v: v.tensor_tensor(
                    deltac[:], sp_r[:], sp_ln[:], ALU.add),
                    reads=["spr", "spln"], writes=["deltac"])
                p.dma("sync", xcchunk[:], xc_dram.ap()[dt_ * 128:(dt_ + 1) * 128,
                                                       tc * 512:tc * 512 + 512],
                      reads=[("xc", b, dt_)], writes=["xcc"])
                p.op("vector", lambda v: v.tensor_tensor(
                    wchunk[:], deltac[:], xcchunk[:], ALU.mult),
                    reads=["deltac", "xcc"], writes=["wchunk"])
                p.op("vector", lambda v, dt_=dt_: v.tensor_scalar(
                    ybuf[:], xcchunk[:], dcol[:, dt_:dt_ + 1], None, ALU.mult),
                    reads=["xcc", "ma_D"], writes=["ybuf"])
                for n in range(NST):
                    cidx = dt_ * NST + n
                    p.op("scalar", lambda s, dt_=dt_, n=n: s.activation(
                        achunk[:], deltac[:], AF.Exp,
                        scale=A_sb[:, dt_ * NST + n: dt_ * NST + n + 1]),
                        reads=["deltac", "ma_A"], writes=["achunk"])
                    p.op("vector", lambda v, n=n: v.tensor_tensor(
                        tmpc[:], wchunk[:], bcb[:, n * 512:(n + 1) * 512],
                        ALU.mult),
                        reads=["wchunk", ("bcb", n)], writes=["tmpc"])
                    p.op("vector", lambda v, cidx=cidx: v.tensor_tensor_scan(
                        schunk[:], achunk[:], tmpc[:],
                        carry[:, cidx:cidx + 1], ALU.mult, ALU.add),
                        reads=["achunk", "tmpc", "carry"], writes=["schunk"])
                    p.op("vector", lambda v, cidx=cidx: v.tensor_copy(
                        carry[:, cidx:cidx + 1], schunk[:, 511:512]),
                        reads=["schunk"], writes=["carry"])
                    p.op("vector", lambda v, n=n: v.tensor_tensor(
                        tmpc[:], schunk[:], bcb[:, (NST + n) * 512:(NST + n + 1) * 512],
                        ALU.mult),
                        reads=["schunk", ("bcb", NST + n)], writes=["tmpc"])
                    p.op("vector", lambda v: v.tensor_tensor(
                        ybuf[:], ybuf[:], tmpc[:], ALU.add),
                        reads=["ybuf", "tmpc"], writes=["ybuf"])
                p.dma("sync", zchunk[:], zT_dram.ap()[dt_ * 128:(dt_ + 1) * 128,
                                                      tc * 512:tc * 512 + 512],
                      reads=[("zT", b, dt_, tc)], writes=["zchunk"])
                p.op("scalar", lambda s: s.activation(sz32[:], zchunk[:], AF.Silu),
                     reads=["zchunk"], writes=["sz32"])
                p.op("vector", lambda v, dt_=dt_: v.tensor_tensor(
                    ybb[:, dt_ * 512:(dt_ + 1) * 512], ybuf[:], sz32[:],
                    ALU.mult),
                    reads=["ybuf", "sz32"], writes=[("ybb", dt_)])
            for sub in range(4):
                it = tc * 4 + sub
                par = sub % 2
                row0 = b * L + it * 128
                p.dma("sync", xt[par][:], xu_dram.ap()[row0:row0 + 128, :],
                      reads=[("xu", b, it)], writes=[(f"xt{par}",)])
                for half in range(2):
                    pairs = [(ybb[:, dt_ * 512 + sub * 128: dt_ * 512 + sub * 128 + 128],
                              woutb[:, dt_ * 1024 + half * 512: dt_ * 1024 + half * 512 + 512],
                              [("ybb", dt_), ("wout", dt_)]) for dt_ in range(NDT)]
                    mm_chain(env, pq[:], "pp1", pairs, tag="woutm")
                    p.op("vector", lambda v, half=half, par=par: v.tensor_tensor(
                        xob[par][:, half * 512:half * 512 + 512], pq[:],
                        xt[par][:, half * 512:half * 512 + 512], ALU.add),
                        reads=["pp1", (f"xt{par}",)], writes=[(f"xo{par}", half)])
                p.dma("scalar", x_out[row0:row0 + 128, :], xob[par][:],
                      reads=[(f"xo{par}", 0), (f"xo{par}", 1)],
                      writes=[("xout", b, it)], tag="xom")


def build_mamba_program(B, L):
    nc = bass.Bass("TRN2", target_bir_lowering=False)
    cfg = {"B": B, "L": L, "T": B * L}
    T = B * L
    x_in = nc.dram_tensor("x_in", [T, D_MODEL], F32, kind="ExternalInput").ap()
    x_out = nc.dram_tensor("x_out", [T, D_MODEL], F32, kind="ExternalOutput").ap()
    w = {k: nc.dram_tensor(k, list(shp), F32, kind="ExternalInput").ap()
         for k, shp in [("W_in", (1024, 4096)), ("ln_g", (1024,)),
                        ("ln_b", (1024,)), ("conv_w", (2048, 4)),
                        ("conv_b", (2048,)), ("W_xproj", (2048, 96)),
                        ("W_dt", (64, 2048)), ("b_dt", (2048,)),
                        ("A_log", (2048, 16)), ("D_param", (2048,)),
                        ("W_out", (2048, 1024))]}
    with ExitStack() as es:
        prog = Prog(nc)
        env = Env(nc, prog, es)
        sems = {e: es.enter_context(nc.semaphore(f"s_{e}")) for e in ENGINES}
        dma_sems = {q: es.enter_context(nc.semaphore(f"d_{q}")) for q in QUEUES}
        ident, ones_col = declare_const_inputs(nc, env)
        emit_mamba(env, cfg, x_in, x_out, w, ident, ones_col)
        prog.op("sync", lambda e: e.nop(),
                reads=[("xout", b, it) for b in range(B) for it in range(L // 128)])
        block = es.enter_context(nc.Block())
        prog.finalize_and_emit(block, sems, dma_sems)
    return nc


def emit_l0_ffn_prep(env, w, ident, ones_col):
    p = env.prog
    L = env.cfg_L
    P2 = 2 * L - 1
    wqg_dram = env.dram_scratch("ff_wqg", [1024, 1024], BF16)
    wkg_dram = env.dram_scratch("ff_wkg", [1024, 1024], BF16)
    wvg_dram = env.dram_scratch("ff_wvg", [1024, 1024], BF16)
    wob_dram = env.dram_scratch("ff_wob", [1024, 1024], BF16)
    keysg_dram = env.dram_scratch("ff_keysg", [32, 1024, 128], BF16)
    valsb_dram = env.dram_scratch("ff_valsb", [32, 128, 1024], BF16)
    prevT_dram = env.dram_scratch("ff_prevT", [1024, P2], BF16)
    bias_dram = env.dram_scratch("ff_biases", [4096], F32)

    wtmp32 = env.sbuf("ma_w32", [128, 1024], F32)
    wtmpb = env.sbuf("ma_wb", [128, 1024], BF16)
    g1col = env.sbuf("ff_g1", [128, NCH], F32)
    b1col = env.sbuf("ff_b1", [128, NCH], BF16)
    g2col = env.sbuf("ff_g2", [128, NCH], F32)
    b2col = env.sbuf("ff_b2", [128, NCH], BF16)
    for (nm, ap_, key) in (("g1", w["ln1_g"], "ffg1"), ("b1", w["ln1_b"], "ffb1"),
                           ("g2", w["ln2_g"], "ffg2"), ("b2", w["ln2_b"], "ffb2")):
        dst = {"g1": g1col, "b1": b1col, "g2": g2col, "b2": b2col}[nm]
        p.dma("gpsimd", dst[:], ap_.rearrange("(c p) -> p c", p=128),
              writes=[key], allow_slow_non_contiguous=True)
    pq = env.psum("pp1", [128, 512], F32)
    pb = env.psum("pp0", [128, 512], F32)
    brow = env.sbuf("ff_brow", [1, 1024], F32)
    for wi, (wsrc, wdst, boff) in enumerate(
            ((w["wq"], wqg_dram, 0), (w["wk"], wkg_dram, 1024),
             (w["wv"], wvg_dram, 2048))):
        for dc in range(NCH):
            p.dma("sync", wtmp32[:], wsrc[dc * 128:(dc + 1) * 128, :],
                  writes=["ma_w32"])
            p.op("vector", lambda v: v.tensor_copy(wtmpb[:], wtmp32[:]),
                 reads=["ma_w32"], writes=["ma_wb"])
            for hh in range(2):
                p.op("tensor", lambda t, dc=dc, hh=hh: t.matmul(
                    (pb if hh == 0 else pq)[0:1, 0:512],
                    b1col[:, dc:dc + 1], wtmpb[:, hh * 512:hh * 512 + 512],
                    start=(dc == 0), stop=(dc == NCH - 1)),
                    reads=["ma_wb", "ffb1"],
                    writes=["pp0" if hh == 0 else "pp1"], tag="fbias")
            p.op("vector", lambda v, dc=dc: v.tensor_scalar(
                wtmpb[:], wtmp32[:], g1col[:, dc:dc + 1], None, ALU.mult),
                reads=["ma_w32", "ffg1", "pp0", "pp1"], writes=["ma_wb"])
            p.dma("scalar", wdst.ap()[dc * 128:(dc + 1) * 128, :], wtmpb[:],
                  reads=["ma_wb"], writes=[(f"wfold{wi}", dc)])
        for hh in range(2):
            p.op("scalar", lambda s, hh=hh: s.activation(
                brow[:, hh * 512:hh * 512 + 512],
                (pb if hh == 0 else pq)[0:1, 0:512], AF.Copy),
                reads=["pp0" if hh == 0 else "pp1"], writes=[("brow", hh)])
        p.dma("scalar", bias_dram.ap()[boff:boff + 1024].rearrange(
            "(o d) -> o d", o=1), brow[:],
            reads=[("brow", 0), ("brow", 1)], writes=[("biasd", wi)])
    for dc in range(NCH):
        p.dma("sync", wtmp32[:], w["wo"][dc * 128:(dc + 1) * 128, :],
              writes=["ma_w32"])
        p.op("vector", lambda v: v.tensor_copy(wtmpb[:], wtmp32[:]),
             reads=["ma_w32"], writes=["ma_wb"])
        p.dma("scalar", wob_dram.ap()[dc * 128:(dc + 1) * 128, :], wtmpb[:],
              reads=["ma_wb"], writes=[("ffwo", dc)])
    selg = env.sbuf("ff_selg", [128, NCH * 32], BF16)
    selb = env.sbuf("ff_selb", [1, 32], BF16)
    seltmp = env.sbuf("ff_seltmp", [128, 32], F32)
    seltb = env.sbuf("ff_seltb", [128, 32], BF16)
    for dc in range(NCH):
        p.dma("sync", seltmp[:], w["sel_w"][dc * 128:(dc + 1) * 128, :],
              writes=["seltmp"])
        p.op("vector", lambda v: v.tensor_copy(seltb[:], seltmp[:]),
             reads=["seltmp"], writes=["seltb"])
        p.op("tensor", lambda t, dc=dc: t.matmul(
            pq[0:1, 0:32], b2col[:, dc:dc + 1], seltb[:],
            start=(dc == 0), stop=(dc == NCH - 1)),
            reads=["seltb", "ffb2"], writes=["pp1"], tag="selb")
        p.op("vector", lambda v, dc=dc: v.tensor_scalar(
            selg[:, dc * 32:(dc + 1) * 32], seltmp[:], g2col[:, dc:dc + 1],
            None, ALU.mult),
            reads=["seltmp", "pp1"], writes=[("selg", dc)])
    p.op("scalar", lambda s: s.activation(selb[:], pq[0:1, 0:32], AF.Copy),
         reads=["pp1"], writes=["selbias"])
    keysb = env.sbuf("ff_keysb", [128, 32], F32)
    kbrow = env.sbuf("ff_kbrow", [1, 128], F32)
    kb_dram = env.dram_scratch("ff_kbias", [32, 128], F32)
    ktmp32 = env.sbuf("ff_kt32", [128, 128], F32)
    ktmpb = env.sbuf("ff_ktb", [128, 128], BF16)
    for e in range(32):
        for dc in range(NCH):
            p.dma("sync", ktmp32[:], w["keys"][e, dc * 128:(dc + 1) * 128, :],
                  writes=["ktmp32"])
            p.op("vector", lambda v: v.tensor_copy(ktmpb[:], ktmp32[:]),
                 reads=["ktmp32"], writes=["ktmpb"])
            p.op("tensor", lambda t, dc=dc: t.matmul(
                pq[0:1, 0:128], b2col[:, dc:dc + 1], ktmpb[:],
                start=(dc == 0), stop=(dc == NCH - 1)),
                reads=["ktmpb", "ffb2"], writes=["pp1"], tag="keyb")
            p.op("vector", lambda v, dc=dc: v.tensor_scalar(
                ktmpb[:], ktmp32[:], g2col[:, dc:dc + 1], None, ALU.mult),
                reads=["ktmp32", "ffg2", "pp1"], writes=["ktmpb"])
            p.dma("scalar", keysg_dram.ap()[e, dc * 128:(dc + 1) * 128, :],
                  ktmpb[:], reads=["ktmpb"], writes=[("keysg", e, dc)])
        p.op("scalar", lambda s: s.activation(kbrow[:], pq[0:1, 0:128],
                                              AF.Copy),
             reads=["pp1"], writes=["kbrow"])
        p.dma("scalar", kb_dram.ap()[e, :].rearrange("(o d) -> o d", o=1),
              kbrow[:], reads=["kbrow"], writes=[("kbd", e)])
    p.dma("gpsimd", keysb[:],
          kb_dram.ap().rearrange("e p -> p e"),
          reads=[("kbd", e) for e in range(32)], writes=["keysb"],
          allow_slow_non_contiguous=True)
    for e in range(32):
        for q4 in range(4):
            p.dma("sync", wtmp32[:, 0:256],
                  w["values"][e, :, q4 * 256:(q4 + 1) * 256], writes=["ma_w32"])
            p.op("vector", lambda v: v.tensor_copy(wtmpb[:, 0:256],
                                                   wtmp32[:, 0:256]),
                 reads=["ma_w32"], writes=["ma_wb"])
            p.dma("scalar", valsb_dram.ap()[e, :, q4 * 256:(q4 + 1) * 256],
                  wtmpb[:, 0:256], reads=["ma_wb"], writes=[("valsb", e, q4)])
    wpb = env.sbuf("ff_wpb", [128, NCH * 128], BF16)
    perow = env.sbuf("ff_perow", [128, NCH * 512], BF16)
    nchk = cdiv(P2, 512)
    for po in range(NCH):
        for dc in range(NCH):
            p.dma("sync", wtmp32[:, 0:128],
                  w["wp"][dc * 128:(dc + 1) * 128, po * 128:(po + 1) * 128],
                  writes=["ma_w32"])
            p.op("vector", lambda v, dc=dc: v.tensor_copy(
                wpb[:, dc * 128:(dc + 1) * 128], wtmp32[:, 0:128]),
                reads=["ma_w32"], writes=[("wpb", dc)])
        for ck in range(nchk):
            n = min(512, P2 - ck * 512)
            for dc in range(NCH):
                p.dma("sync", perow[:, dc * 512:dc * 512 + n],
                      env.perevT_dram[dc * 128:(dc + 1) * 128,
                                      ck * 512:ck * 512 + n],
                      writes=[("perow", dc)])
            pairs = [(wpb[:, dc * 128:(dc + 1) * 128],
                      perow[:, dc * 512:dc * 512 + n],
                      [("wpb", dc), ("perow", dc)]) for dc in range(NCH)]
            mm_chain(env, pq[:, 0:n], "pp1", pairs, tag="prev")
            p.op("scalar", lambda s, n=n: s.activation(
                wtmpb[:, 0:n], pq[:, 0:n], AF.Copy),
                reads=["pp1"], writes=["ma_wb"])
            p.dma("scalar", prevT_dram.ap()[po * 128:(po + 1) * 128,
                                            ck * 512:ck * 512 + n],
                  wtmpb[:, 0:n], reads=["ma_wb"], writes=[("prevT", po, ck)])
    return dict(wqg=wqg_dram, wkg=wkg_dram, wvg=wvg_dram, wob=wob_dram,
                keysg=keysg_dram, valsb=valsb_dram, prevT=prevT_dram,
                biases=bias_dram, selg=selg, selb=selb, keysb=keysb)


def emit_l0_ffn(env, cfg, x_in, x_out, prep, ident, ones_col, call_id):
    p = env.prog
    B, L, T = cfg["B"], cfg["L"], cfg["T"]
    ntb = L // 128
    ntc4 = T // 256
    H = 16
    P2 = 2 * L - 1
    W = L + 127
    S = L + 128
    SCALE = 1.0 / 8.0
    u = call_id

    xt = [env.sbuf(f"xt{i}", [128, D_MODEL], F32) for i in range(2)]
    srt = [env.sbuf(f"srt{i}", [128, D_MODEL], F32) for i in range(2)]
    ht = [env.sbuf(f"ht{i}", [128, D_MODEL], BF16) for i in range(2)]
    hTt = [env.sbuf(f"hTt{i}", [128, D_MODEL], BF16) for i in range(2)]
    xob = [env.sbuf(f"xo{i}", [128, D_MODEL], F32) for i in range(2)]
    stats = env.sbuf("stats", [128, 16], F32)
    stats2 = env.sbuf("stats2", [128, 16], F32)
    scratch = [env.sbuf(f"lnscratch{i}", [128, D_MODEL], F32) for i in range(2)]

    wphase = env.sbuf("ff_wphase", [128, 2 * NCH * 1024], BF16)
    bqkv = env.sbuf("ff_bqkv", [128, 3 * NCH], F32)
    p.dma("gpsimd", bqkv[:],
          prep["biases"].ap()[0:3072].rearrange("(w c p) -> p (w c)", p=128, w=3),
          reads=[("biasd", 0), ("biasd", 1), ("biasd", 2)], writes=["bqkv"],
          allow_slow_non_contiguous=True)
    qT = env.sbuf("qb_qT", [128, NCH * L], BF16)
    kbuf = env.sbuf("ff_kbuf", [128, L], BF16)
    vbuf = env.sbuf("ff_vbuf", [128, ntb * 64], BF16)
    pbuf = env.sbuf("ff_pbuf", [128, P2], BF16)
    esb = [env.sbuf(f"ab_e{i}", [128, L], BF16) for i in range(2)]
    etsb = [env.sbuf(f"ab_et{i}", [128, 256], BF16) for i in range(2)]
    bdstage = env.sbuf("ff_bdstage", [128, 512], BF16)
    vstage = env.sbuf("ff_vstage", [128, 1024], BF16)
    vb_row = env.sbuf("ff_vbrow", [1, 1024], BF16)
    p.dma("gpsimd", vb_row[:],
          prep["biases"].ap()[2048:3072].rearrange("(o d) -> o d", o=1),
          reads=[("biasd", 2)], writes=["vbrow"])
    env.ffn_vbias_row = vb_row
    bdskew = env.sbuf("ff_bdskew", [128, L], BF16)
    zcols = env.sbuf("ab_z", [128, 16], F32)
    rzb = env.sbuf("ab_rz", [128, 512], F32)
    gateT = env.sbuf("ff_gateT", [32, T], BF16)
    self32 = env.sbuf("ff_sel", [128, 32], F32)
    m8 = env.sbuf("aa_m8", [128, 8], F32)
    gateb = env.sbuf("ff_gateb", [128, 32], BF16)
    hgrid = env.sbuf("ff_hgrid", [128, 32 * 256], BF16)
    hrelu = env.sbuf("ff_hrelu", [128, 256], BF16)
    gbc = env.sbuf("ff_gbc", [128, 256], BF16)
    x2Tc = env.sbuf("ff_x2Tc", [128, NCH * 256], BF16)
    keb = [env.sbuf("ff_keb0", [128, NCH * 128], BF16)] * 2
    veb = [env.sbuf("ff_veb0", [128, 1024], BF16)] * 2

    spp = env.psum("spp", [128, 2048], F32)
    tpp = env.psum("tpp", [128, 512], BF16)
    pvp = env.psum("pvp", [128, 512], F32)
    pq = env.psum("pp1", [128, 512], F32)
    wpp = env.psum("pp0", [128, 512], F32)

    ktf_dram = env.dram_scratch("ff_ktf", [1024, L], BF16)
    vtok_dram = env.dram_scratch("ff_vtok", [L, 1024], BF16)
    src_dram = env.dram_scratch("ff_src", [T, D_MODEL], F32)
    xs2_dram = env.dram_scratch("ff_xs2", [T, D_MODEL], F32)
    x2T_dram = env.dram_scratch("ff_x2T", [1024, T], BF16)
    gate_dram = env.dram_scratch("ff_gate", [32, T], BF16)
    bdskew_dram = env.dram_scratch("ff_bdsk", [128 * (L + 128)], BF16)
    zrow_dram = env.dram_scratch("zrow", [2048], F32)

    h2_dram = env.dram_scratch("ff_h2", [L, D_MODEL], BF16)
    for b in range(B):
        for wi, nm in enumerate(("wqg", "wkg")):
            for dc in range(NCH):
                p.dma("sync",
                      wphase[:, (wi * NCH + dc) * 1024:(wi * NCH + dc + 1) * 1024],
                      prep[nm].ap()[dc * 128:(dc + 1) * 128, :],
                      reads=[(f"wfold{wi}", dc)], writes=[("wph", wi, dc)])
        for it in range(ntb):
            par = it % 2
            row0 = b * L + it * 128
            p.dma("sync", xt[par][:], x_in[row0:row0 + 128, :], writes=[(f"xt{par}",)])
            emit_ln_tile(env, xt[par], (f"xt{par}",), srt[par], (f"sr{par}",),
                         stats[:, 8 * par:8 * par + 8], f"st{par}",
                         scratch[par], f"lnscratch{par}", D_MODEL)
            p.dma("scalar", src_dram.ap()[row0:row0 + 128, :], srt[par][:],
                  reads=[(f"sr{par}",)], writes=[("src", b, it)])
            emit_ln_tile(env, srt[par], (f"sr{par}",), ht[par], (f"ht{par}",),
                         stats2[:, 8 * par:8 * par + 8], f"s2t{par}",
                         scratch[par], f"lnscratch{par}", D_MODEL)
            emit_transpose_tile(env, ht[par], (f"ht{par}",), hTt[par], f"hT{par}",
                                ident, NCH)
            hkeys = [(f"hT{par}", g) for g in range(2)]
            for wi, dl in ((0, None), (1, "k")):
                for oc in range(NCH):
                    pairs = [(wphase[:, (wi * NCH + dc) * 1024 + oc * 128:
                                     (wi * NCH + dc) * 1024 + oc * 128 + 128],
                              hTt[par][:, dc * 128:(dc + 1) * 128],
                              [("wph", wi, dc)] + hkeys) for dc in range(NCH)]
                    mm_chain(env, pq[:, 0:128], "pp1", pairs, tag="fqk")
                    if wi == 0:
                        p.op("scalar", lambda s, oc=oc, it=it: s.activation(
                            qT[:, oc * L + it * 128: oc * L + it * 128 + 128],
                            pq[:, 0:128], AF.Identity,
                            bias=bqkv[:, oc:oc + 1]),
                            reads=["pp1", "bqkv"], writes=[("qT", oc, it)])
                    else:
                        p.op("scalar", lambda s, oc=oc: s.activation(
                            bdstage[:, 0:128], pq[:, 0:128], AF.Identity,
                            bias=bqkv[:, NCH + oc:NCH + oc + 1]),
                            reads=["pp1", "bqkv"], writes=["bdstage"])
                        p.dma("scalar",
                              ktf_dram.ap()[oc * 128:(oc + 1) * 128,
                                            it * 128:(it + 1) * 128],
                              bdstage[:, 0:128], reads=["bdstage"],
                              writes=[("ktf", oc, it)])
            p.dma("scalar", h2_dram.ap()[it * 128:(it + 1) * 128, :],
                  hTt[par][:], reads=hkeys, writes=[("h2d", it)])
        for dc in range(NCH):
            p.dma("sync", wphase[:, dc * 1024:(dc + 1) * 1024],
                  prep["wvg"].ap()[dc * 128:(dc + 1) * 128, :],
                  reads=[("wfold2", dc)], writes=[("wph", 0, dc)])
        for it in range(ntb):
            par = it % 2
            p.dma("sync", hTt[par][:], h2_dram.ap()[it * 128:(it + 1) * 128, :],
                  reads=[("h2d", it)], writes=[(f"hT{par}", 0), (f"hT{par}", 1)])
            hkeys = [(f"hT{par}", g) for g in range(2)]
            for half in range(2):
                pairs = [(hTt[par][:, dc * 128:(dc + 1) * 128],
                          wphase[:, dc * 1024 + half * 512:
                                 dc * 1024 + half * 512 + 512],
                          [("wph", 0, dc)] + hkeys) for dc in range(NCH)]
                mm_chain(env, wpp[:], "pp0", pairs, stop_last=False, tag="fv")
                p.op("tensor", lambda t, half=half: t.matmul(
                    wpp[:], ones_col[:], env.ffn_vbias_row[:, half * 512:half * 512 + 512],
                    start=False, stop=True),
                    reads=["onescol", "vbrow"], writes=["pp0"])
                p.op("scalar", lambda s, half=half: s.activation(
                    vstage[:, half * 512:half * 512 + 512], wpp[:], AF.Copy),
                    reads=["pp0"], writes=[("vstage", half)])
            p.dma("scalar", vtok_dram.ap()[it * 128:(it + 1) * 128, :],
                  vstage[:], reads=[("vstage", 0), ("vstage", 1)],
                  writes=[("vtok", it)])
        for h in range(H):
            oc, po = h // 2, 64 * (h % 2)
            p.dma("sync", kbuf[po:po + 64, :],
                  ktf_dram.ap()[h * 64:(h + 1) * 64, :],
                  reads=[("ktf", oc, it2) for it2 in range(ntb)],
                  writes=["kbuf"])
            p.dma("sync", pbuf[po:po + 64, :],
                  prep["prevT"].ap()[h * 64:(h + 1) * 64, :],
                  reads=[("prevT", oc, ck) for ck in range(cdiv(P2, 512))],
                  writes=["pbuf"])
            p.dma("sync", vbuf[:, :],
                  bass.AP(vtok_dram.ap().tensor, h * 64,
                          [[1024, 128], [128 * 1024, ntb], [1, 64]]),
                  reads=[("vtok", it2) for it2 in range(ntb)],
                  writes=["vbuf"])
            for ig in range(ntb // 2):
                for i2 in range(2):
                    it = ig * 2 + i2
                    m0 = L - 128 - 128 * it
                    nw = cdiv(W, 512)
                    for ck in range(nw):
                        n = min(512, W - ck * 512)
                        p.op("tensor", lambda t, oc=oc, po=po, it=it, ck=ck,
                             n=n, m0=m0: t.matmul(
                                 wpp[:, 0:n],
                                 qT[po:po + 64, oc * L + it * 128: oc * L + it * 128 + 128],
                                 pbuf[po:po + 64, m0 + ck * 512: m0 + ck * 512 + n],
                                 start=True, stop=True),
                             reads=[("qT", oc, it), "pbuf"], writes=["pp0"],
                             tag="bdw")
                        p.op("scalar", lambda s, n=n: s.activation(
                            bdstage[:, 0:n], wpp[:, 0:n], AF.Copy),
                            reads=["pp0"], writes=["bdstage"])
                        p.dma("scalar",
                              bass.AP(bdskew_dram.ap().tensor, ck * 512,
                                      [[S, 128], [1, n]]),
                              bdstage[:, 0:n], reads=["bdstage"],
                              writes=[("bdd", ck)])
                    p.dma("sync", bdskew[:],
                          bass.AP(bdskew_dram.ap().tensor, 127,
                                  [[S - 1, 128], [1, L]]),
                          reads=[("bdd", ck) for ck in range(nw)],
                          writes=["bdskew"])
                    for jc in range(L // 512):
                        pairs = [(qT[po:po + 64, oc * L + it * 128: oc * L + it * 128 + 128],
                                  kbuf[po:po + 64, jc * 512:(jc + 1) * 512],
                                  [("qT", oc, it), "kbuf"])]
                        mm_chain(env, spp[:, jc * 512:(jc + 1) * 512],
                                 ("spp", jc), pairs, stop_last=False, tag="scf")
                        p.op("tensor", lambda t, jc=jc: t.matmul(
                            spp[:, jc * 512:(jc + 1) * 512], ident[:],
                            bdskew[:, jc * 512:(jc + 1) * 512],
                            start=False, stop=True),
                            reads=["ident", "bdskew"], writes=[("spp", jc)],
                            tag="bdadd")
                    s_m = stats[:, 0:1]
                    s_negm = stats[:, 1:2]
                    spkeys = [("spp", jc) for jc in range(L // 512)]
                    p.op("vector", lambda v: v.tensor_reduce(
                        s_m, spp[:, 0:L], AX.X, ALU.max),
                        reads=spkeys, writes=[("stats", 0)])
                    p.op("vector", lambda v: v.tensor_scalar(
                        s_negm, s_m, -SCALE, None, ALU.mult),
                        reads=[("stats", 0)], writes=[("stats", 1)])
                    p.op("scalar", lambda s, i2=i2, it=it: s.activation(
                        esb[i2][:, 0:L], spp[:, 0:L], AF.Exp, scale=SCALE,
                        bias=s_negm, accum_out=zcols[:, it:it + 1]),
                        reads=spkeys + [("stats", 1)],
                        writes=[("e", i2), ("zc", it)])
                p.dma("gpsimd",
                      bass.AP(zrow_dram.ap().tensor, 0, [[1, 128], [128, 2]]),
                      zcols[:, ig * 2:ig * 2 + 2],
                      reads=[("zc", ig * 2 + i) for i in range(2)],
                      writes=["zrowd"], allow_slow_non_contiguous=True)
                p.dma("gpsimd", rzb[po:po + 64, 0:256],
                      bass.AP(zrow_dram.ap().tensor, 0, [[0, 64], [1, 256]]),
                      reads=["zrowd"], writes=["rzb"])
                p.op("vector", lambda v, po=po: v.reciprocal(
                    rzb[po:po + 64, 0:256], rzb[po:po + 64, 0:256]),
                     reads=["rzb"], writes=["rzb"])
                for jt in range(ntb):
                    for i2 in range(2):
                        p.op("tensor", lambda t, i2=i2, jt=jt: t.transpose(
                            tpp[:].bitcast(BF16)[:, i2 * 128:(i2 + 1) * 128],
                            esb[i2][:, jt * 128:(jt + 1) * 128], ident[:]),
                            reads=[("e", i2), "ident"], writes=["tpp"])
                    eb = jt % 2
                    p.op("scalar", lambda s, eb=eb: s.activation(
                        etsb[eb][:], tpp[:].bitcast(BF16)[:, 0:256], AF.Copy),
                        reads=["tpp"], writes=[("et", eb)])
                    p.op("tensor", lambda t, jt=jt, eb=eb, po=po: t.matmul(
                        pvp[po:po + 64, 0:256], vbuf[:, jt * 64:(jt + 1) * 64],
                        etsb[eb][:], start=(jt == 0), stop=(jt == ntb - 1)),
                        reads=["vbuf", ("et", eb)], writes=["pvp"], tag="pvf")
                p.op("vector", lambda v, oc=oc, po=po, ig=ig: v.tensor_tensor(
                    qT[po:po + 64, oc * L + ig * 256: oc * L + ig * 256 + 256],
                    pvp[po:po + 64, 0:256], rzb[po:po + 64, 0:256], ALU.mult),
                    reads=["pvp", "rzb"], writes=[("attnT", h, ig)])
        for dc in range(NCH):
            p.dma("sync", wphase[:, dc * 1024:(dc + 1) * 1024],
                  prep["wob"].ap()[dc * 128:(dc + 1) * 128, :],
                  reads=[("ffwo", dc)], writes=[("wph", 0, dc)])
        for it in range(ntb):
            par = it % 2
            row0 = b * L + it * 128
            akeys = [("attnT", hh, it // 2) for hh in range(H)]
            p.dma("sync", xt[par][:], x_in[row0:row0 + 128, :], writes=[(f"xt{par}",)])
            p.dma("sync", srt[par][:], src_dram.ap()[row0:row0 + 128, :],
                  reads=[("src", b, it)], writes=[(f"sr{par}",)])
            for half in range(2):
                pairs = [(qT[:, dc * L + it * 128: dc * L + it * 128 + 128],
                          wphase[:, dc * 1024 + half * 512: dc * 1024 + half * 512 + 512],
                          akeys + [("wph", 0, dc)]) for dc in range(NCH)]
                mm_chain(env, pq[:], "pp1", pairs, tag="fwo")
                p.op("vector", lambda v, half=half, par=par: v.tensor_tensor(
                    scratch[par][:, half * 512:half * 512 + 512], pq[:],
                    srt[par][:, half * 512:half * 512 + 512], ALU.add),
                    reads=["pp1", (f"sr{par}",)], writes=[(f"lnscratch{par}", half)])
                p.op("vector", lambda v, half=half, par=par: v.tensor_tensor(
                    xob[par][:, half * 512:half * 512 + 512],
                    scratch[par][:, half * 512:half * 512 + 512],
                    xt[par][:, half * 512:half * 512 + 512], ALU.add),
                    reads=[(f"lnscratch{par}", half), (f"xt{par}",)],
                    writes=[(f"xo{par}", half)])
            p.dma("scalar", xs2_dram.ap()[row0:row0 + 128, :], xob[par][:],
                  reads=[(f"xo{par}", 0), (f"xo{par}", 1)], writes=[("xs2", b, it)])
            emit_ln_tile(env, scratch[par],
                         [(f"lnscratch{par}", 0), (f"lnscratch{par}", 1)],
                         ht[par], (f"ht{par}",),
                         stats2[:, 8 * par:8 * par + 8], f"s2t{par}",
                         srt[par], (f"sr{par}",), D_MODEL)
            emit_transpose_tile(env, ht[par], (f"ht{par}",), hTt[par], f"hT{par}",
                                ident, NCH)
            gidx = b * ntb + it
            p.dma("scalar",
                  x2T_dram.ap().rearrange("(c p) t -> p c t", p=128)[
                      :, :, gidx * 128:gidx * 128 + 128],
                  hTt[par][:].rearrange("p (c n) -> p c n", n=128),
                  reads=[(f"hT{par}", g) for g in range(2)],
                  writes=[("x2T", b, it)])
            pairs = [(hTt[par][:, dc * 128:(dc + 1) * 128],
                      prep["selg"][:, dc * 32:(dc + 1) * 32],
                      [(f"hT{par}", g) for g in range(2)] + [("selg", dc)])
                     for dc in range(NCH)]
            mm_chain(env, pq[:, 0:32], "pp1", pairs, stop_last=False, tag="sel")
            p.op("tensor", lambda t: t.matmul(
                pq[:, 0:32], ones_col[:], prep["selb"][:], start=False,
                stop=True), reads=["onescol", "selbias"], writes=["pp1"])
            p.op("scalar", lambda s: s.activation(self32[:], pq[:, 0:32],
                                                  AF.Sigmoid),
                 reads=["pp1"], writes=["self32"])
            p.op("vector", lambda v: v.max(m8[:], self32[:]),
                 reads=["self32"], writes=["m8"])
            p.op("vector", lambda v: v.scalar_tensor_tensor(
                gateb[:], self32[:], m8[:, 3:4], self32[:], ALU.is_ge, ALU.mult),
                reads=["self32", "m8"], writes=["gateb"])
            p.op("tensor", lambda t: t.transpose(
                tpp[:].bitcast(BF16)[0:32, 0:128], gateb[:], ident[:]),
                reads=["gateb", "ident"], writes=["tpp"])
            p.op("scalar", lambda s, gidx=gidx: s.activation(
                gateT[0:32, gidx * 128:(gidx + 1) * 128],
                tpp[:].bitcast(BF16)[0:32, 0:128], AF.Copy),
                reads=["tpp"], writes=[("gateT", gidx)])
    p.dma("scalar", gate_dram.ap()[:, :], gateT[0:32, :],
          reads=[("gateT", g) for g in range(T // 128)], writes=["gated"])
    for tc in range(ntc4):
        p.dma("sync", x2Tc[:],
              x2T_dram.ap().rearrange("(c p) t -> p c t", p=128)[
                  :, :, tc * 256:(tc + 1) * 256],
              reads=[("x2T", bb, itt) for bb in range(B) for itt in range(ntb)],
              writes=["x2Tc"])
        for e in range(32):
            kb = keb[e % 2]
            for dc in range(NCH):
                p.dma("sync", kb[:, dc * 128:(dc + 1) * 128],
                      prep["keysg"].ap()[e, dc * 128:(dc + 1) * 128, :],
                      reads=[("keysg", e, dc)], writes=[("keb", 0)])
            pairs = [(kb[:, dc * 128:(dc + 1) * 128],
                      x2Tc[:, dc * 256:(dc + 1) * 256],
                      [("keb", 0), "x2Tc"]) for dc in range(NCH)]
            mm_chain(env, wpp[:, 0:256], "pp0", pairs, tag="moek")
            p.op("scalar", lambda s, e=e: s.activation(
                hrelu[:, 0:256], wpp[:, 0:256], AF.Relu,
                bias=prep["keysb"][:, e:e + 1]),
                reads=["pp0", "keysb"], writes=["hrelu"])
            p.dma("gpsimd", gbc[:, 0:256],
                  bass.AP(gate_dram.ap().tensor, e * T + tc * 256,
                          [[0, 128], [1, 256]]),
                  reads=["gated"], writes=["gbc"])
            p.op("vector", lambda v, e=e: v.tensor_tensor(
                hgrid[:, e * 256:(e + 1) * 256], hrelu[:, 0:256], gbc[:, 0:256],
                ALU.mult),
                reads=["hrelu", "gbc"], writes=[("hgrid", e)])
        for e in range(32):
            vb = veb[e % 2]
            p.dma("sync", vb[:], prep["valsb"].ap()[e, :, :],
                  reads=[("valsb", e, q4) for q4 in range(4)],
                  writes=[("veb", 0)])
            for sub in range(2):
                for half in range(2):
                    p.op("tensor", lambda t, e=e, sub=sub, half=half, vb=vb: t.matmul(
                        spp[:, (sub * 2 + half) * 512:(sub * 2 + half + 1) * 512],
                        hgrid[:, e * 256 + sub * 128: e * 256 + sub * 128 + 128],
                        vb[:, half * 512:half * 512 + 512],
                        start=(e == 0), stop=(e == 31)),
                        reads=[("hgrid", e), ("veb", 0)],
                        writes=[("spp", sub * 2 + half)], tag="moev")
        for sub in range(2):
            it = (tc * 256 + sub * 128) // 128
            b_of = it // ntb
            par = sub % 2
            row0 = tc * 256 + sub * 128
            p.dma("sync", xt[par][:], xs2_dram.ap()[row0:row0 + 128, :],
                  reads=[("xs2", b_of, it % ntb)], writes=[(f"xt{par}",)])
            for half in range(2):
                p.op("vector", lambda v, sub=sub, half=half, par=par: v.tensor_tensor(
                    xob[par][:, half * 512:half * 512 + 512],
                    spp[:, (sub * 2 + half) * 512:(sub * 2 + half + 1) * 512],
                    xt[par][:, half * 512:half * 512 + 512], ALU.add),
                    reads=[("spp", sub * 2 + half), (f"xt{par}",)],
                    writes=[(f"xo{par}", half)])
            p.dma("scalar", x_out[row0:row0 + 128, :], xob[par][:],
                  reads=[(f"xo{par}", 0), (f"xo{par}", 1)],
                  writes=[("xout", u, tc, sub)], tag="xoff")


def build_l0_ffn_program(B, L, ncalls=1):
    nc = bass.Bass("TRN2", target_bir_lowering=False)
    cfg = {"B": B, "L": L, "T": B * L}
    T = B * L
    x_in = nc.dram_tensor("x_in", [T, D_MODEL], F32, kind="ExternalInput").ap()
    x_out = nc.dram_tensor("x_out", [T, D_MODEL], F32, kind="ExternalOutput").ap()
    w = {k: nc.dram_tensor(k, list(shp), F32, kind="ExternalInput").ap()
         for k, shp in [("wq", (1024, 1024)), ("wk", (1024, 1024)),
                        ("wv", (1024, 1024)), ("wp", (1024, 1024)),
                        ("wo", (1024, 1024)), ("ln1_g", (1024,)),
                        ("ln1_b", (1024,)), ("ln2_g", (1024,)),
                        ("ln2_b", (1024,)), ("sel_w", (1024, 32)),
                        ("keys", (32, 1024, 128)), ("values", (32, 128, 1024))]}
    with ExitStack() as es:
        prog = Prog(nc)
        env = Env(nc, prog, es)
        env.cfg_L = L
        sems = {e: es.enter_context(nc.semaphore(f"s_{e}")) for e in ENGINES}
        dma_sems = {q: es.enter_context(nc.semaphore(f"d_{q}")) for q in QUEUES}
        env.perevT_dram = nc.dram_tensor("c_perevT", [1024, 2 * L - 1], BF16,
                                         kind="ExternalInput").ap()
        ident, ones_col = declare_const_inputs(nc, env)
        prep = emit_l0_ffn_prep(env, w, ident, ones_col)
        xmid = env.dram_scratch("xmid", [T, D_MODEL], F32).ap()
        for call in range(ncalls):
            src = x_in if call == 0 else xmid
            dst = x_out if call == ncalls - 1 else xmid
            emit_l0_ffn(env, cfg, src, dst, prep, ident, ones_col, call)
        prog.op("sync", lambda e: e.nop(),
                reads=[("xout", ncalls - 1, tc, sub)
                       for tc in range(T // 256) for sub in range(2)])
        block = es.enter_context(nc.Block())
        prog.finalize_and_emit(block, sems, dma_sems)
    return nc


def perev_table(L):
    Ss, Ds = L, 1024
    pos = np.arange(Ss - 1, -Ss, -1, dtype=np.float64)
    inv_freq = 1.0 / 10000.0 ** (np.arange(0, Ds, 2) / Ds)
    ang = pos[:, None] * inv_freq[None, :]
    pe = np.concatenate([np.sin(ang), np.cos(ang)], axis=-1)
    pe_rev = pe[::-1]
    return np.ascontiguousarray(pe_rev.T).astype(BF)


_B, _L = 2, 2048
_PROGS = {}


def _get_prog(name):
    if name not in _PROGS:
        builders = {
            "attn_a": lambda: build_l0_attn_a_program(_B, _L),
            "ffn0": lambda: build_l0_ffn_program(_B, _L),
            "attn_b": lambda: build_l0_attn_b_program(_B, _L),
            "mamba": lambda: build_mamba_program(_B, _L),
            "l1ffn": lambda: build_l1_ffn_program(_B * _L),
            "l1attnb": lambda: build_l1_attn_b_program(_B, _L),
        }
        nc = builders[name]()
        nc.finalize()
        _PROGS[name] = nc
    return _PROGS[name]


def _consts_full():
    c = host_constants()
    cos, sinS = rope_tables(_L)
    c["c_ropecos"] = cos
    c["c_ropesin"] = sinS
    c["c_rk"] = np.tile(1.0 / np.arange(1, 33), 16)[None, :].repeat(128, 0).astype(np.float32)
    c["c_negbig"] = np.full((128, 512), -1e30, np.float32)
    c["c_perevT"] = perev_table(_L)
    c["c_ropeswap"] = rope_swap_matrix()
    return c


def _run(name, in_maps):
    nc = _get_prog(name)
    res = run_bass_kernel_spmd(nc, in_maps, core_ids=list(range(8)))
    return [r["x_out"] for r in res.results]


def kernel(**inputs):
    f = {k: np.ascontiguousarray(np.asarray(v, dtype=np.float32))
         for k, v in inputs.items()}
    C = _consts_full()
    x = f["L0_attn_a_query"].reshape(16, _L, D_MODEL)
    shards = [np.ascontiguousarray(x[i * _B:(i + 1) * _B].reshape(_B * _L, D_MODEL))
              for i in range(8)]

    def maps(wdict, extra_consts):
        base = dict(wdict)
        for k in extra_consts:
            base[k] = C[k]
        return [dict(base, x_in=s) for s in shards]

    w = dict(Wq=f["L0_attn_a_Wq"], Wk=f["L0_attn_a_Wk"], Wv=f["L0_attn_a_Wv"],
             Wo=f["L0_attn_a_Wo"], bq=f["L0_attn_a_bq"], bk=f["L0_attn_a_bk"],
             bv=f["L0_attn_a_bv"], bo=f["L0_attn_a_bo"],
             kpT=np.ascontiguousarray(f["L0_attn_a_key_param"][0].T))
    shards = _run("attn_a", maps(w, ["c_ident", "c_ones_col", "c_rk", "c_negbig"]))
    wffn = dict(wq=f["L0_ffn_wq"], wk=f["L0_ffn_wk"], wv=f["L0_ffn_wv"],
                wp=f["L0_ffn_wp"], wo=f["L0_ffn_wo"], ln1_g=f["L0_ffn_ln1_g"],
                ln1_b=f["L0_ffn_ln1_b"], ln2_g=f["L0_ffn_ln2_g"],
                ln2_b=f["L0_ffn_ln2_b"], sel_w=f["L0_ffn_sel_w"],
                keys=f["L0_ffn_keys"], values=f["L0_ffn_values"])
    shards = _run("ffn0", maps(wffn, ["c_ident", "c_ones_col", "c_perevT"]))
    perm = attn_b_head_perm()
    w = dict(Wq=np.ascontiguousarray(f["L0_attn_b_Wq"][:, perm]),
             Wkv=f["L0_attn_b_Wkv"],
             Wo=np.ascontiguousarray(f["L0_attn_b_Wo"][perm, :]),
             bo=f["L0_attn_b_bo"])
    shards = _run("attn_b", maps(w, ["c_ident", "c_ones_col", "c_ropecos",
                                     "c_ropesin", "c_ropeswap"]))
    shards = _run("ffn0", maps(wffn, ["c_ident", "c_ones_col", "c_perevT"]))
    w = dict(W_in=f["L1_attn_a_W_in"], ln_g=f["L1_attn_a_ln_g"],
             ln_b=f["L1_attn_a_ln_b"],
             conv_w=np.ascontiguousarray(f["L1_attn_a_conv_w"].reshape(2048, 4)),
             conv_b=f["L1_attn_a_conv_b"], W_xproj=f["L1_attn_a_W_xproj"],
             W_dt=f["L1_attn_a_W_dt"], b_dt=f["L1_attn_a_b_dt"],
             A_log=f["L1_attn_a_A_log"], D_param=f["L1_attn_a_D_param"],
             W_out=f["L1_attn_a_W_out"])
    shards = _run("mamba", maps(w, ["c_ident", "c_ones_col"]))
    wl1 = dict(Wt=np.ascontiguousarray(f["L1_ffn_W"].T),
               lora_up=f["L1_ffn_lora_up"],
               up_aux_T=np.ascontiguousarray(f["L1_ffn_up_aux"].T),
               lora_down=f["L1_ffn_lora_down"], down_aux=f["L1_ffn_down_aux"],
               bvec=np.ascontiguousarray(f["L1_ffn_b"][None, :]))
    shards = _run("l1ffn", maps(wl1, ["c_ident", "c_ones_col"]))
    w = dict(Wq=f["L1_attn_b_Wq"], Wk=f["L1_attn_b_Wk"], Wv=f["L1_attn_b_Wv"],
             bq=f["L1_attn_b_bq"], bk=f["L1_attn_b_bk"], bv=f["L1_attn_b_bv"])
    shards = _run("l1attnb", maps(w, ["c_ident", "c_ones_col", "c_tril"]))
    shards = _run("l1ffn", maps(wl1, ["c_ident", "c_ones_col"]))

    out = np.concatenate([s.reshape(_B, _L, D_MODEL) for s in shards], axis=0)
    return out.astype(np.float32)
